# version 10
# baseline (speedup 1.0000x reference)
import contextlib
import numpy as np
import ml_dtypes
import concourse.bass as bass
import concourse.mybir as mybir
from concourse.bass_utils import run_bass_kernel_spmd
from concourse.alu_op_type import AluOpType as A

F32 = mybir.dt.float32
BF16 = mybir.dt.bfloat16
AF = mybir.ActivationFunctionType
AX = mybir.AxisListType.X

D = 1024
FI = 2816
NKC = D // 128
NIC = FI // 128


def _quant_chain(v, u_b, s_small, nb, outcols, cA, cB, ab, sg2):
    F = nb * 16
    v.tensor_tensor(out=ab[:, :F], in0=u_b[:, :F], in1=u_b[:, :F], op=A.mult)
    v.tensor_scalar(out=sg2[:, :F], in0=u_b[:, :F], scalar1=0.0, scalar2=2.0,
                    op0=A.is_gt, op1=A.mult)
    v.tensor_single_scalar(out=cA[:, :F], in_=ab[:, :F], scalar=0.0625,
                           op=A.is_gt)
    for t2 in (0.5625, 1.5625, 3.0625):
        v.scalar_tensor_tensor(out=cA[:, :F], in0=ab[:, :F], scalar=t2,
                               in1=cA[:, :F], op0=A.is_gt, op1=A.add)
    v.tensor_scalar(out=cB[:, :F], in0=ab[:, :F], scalar1=25.0, scalar2=2.0,
                    op0=A.is_gt, op1=A.mult)
    for t2 in (12.25, 6.25):
        v.scalar_tensor_tensor(out=cB[:, :F], in0=ab[:, :F], scalar=t2,
                               in1=cB[:, :F], op0=A.is_gt, op1=A.add)
    v.scalar_tensor_tensor(out=cA[:, :F], in0=cA[:, :F], scalar=0.5,
                           in1=cB[:, :F], op0=A.mult, op1=A.add)
    v.scalar_tensor_tensor(out=cA[:, :F], in0=sg2[:, :F], scalar=-1.0,
                           in1=cA[:, :F], op0=A.add, op1=A.mult)
    sb3 = s_small[:, :nb].unsqueeze(2).broadcast_to([128, nb, 16])
    return v.tensor_tensor(
        out=outcols, in0=cA[:, :F].rearrange("p (nb b) -> p nb b", b=16),
        in1=sb3, op=A.mult)


def _quant_head(v, src3, m_s, s_s, r_s, nb, u_b, vbar):
    vbar(v.tensor_reduce(out=m_s[:, :nb], in_=src3, axis=AX, op=A.max,
                         apply_absolute_value=True))
    vbar(v.tensor_scalar(out=s_s[:, :nb], in0=m_s[:, :nb], scalar1=1.0 / 6.0,
                         scalar2=1e-12, op0=A.mult, op1=A.max))
    vbar(v.reciprocal(out=r_s[:, :nb], in_=s_s[:, :nb]))
    return v.tensor_tensor(
        out=u_b[:, :nb * 16].rearrange("p (nb b) -> p nb b", b=16), in0=src3,
        in1=r_s[:, :nb].unsqueeze(2).broadcast_to([128, nb, 16]), op=A.mult)


def build(S_tok=4096, dbg=False, serial=0):
    NT = S_tok // 128
    nc = bass.Bass("TRN2", target_bir_lowering=False, detect_race_conditions=False)
    KIND = "ExternalOutput" if dbg else "Internal"


    x_d = nc.dram_tensor("x", [S_tok, D], F32, kind="ExternalInput").ap()
    wq_d = nc.dram_tensor("wq", [D, D], F32, kind="ExternalInput").ap()
    wk_d = nc.dram_tensor("wk", [D, D], F32, kind="ExternalInput").ap()
    wv_d = nc.dram_tensor("wv", [D, D], F32, kind="ExternalInput").ap()
    wo_d = nc.dram_tensor("wo", [D, D], F32, kind="ExternalInput").ap()
    wg_d = nc.dram_tensor("wg", [FI, D], F32, kind="ExternalInput").ap()
    wu_d = nc.dram_tensor("wu", [FI, D], F32, kind="ExternalInput").ap()
    wd_d = nc.dram_tensor("wd", [D, FI], F32, kind="ExternalInput").ap()
    id_d = nc.dram_tensor("ident", [128, 128], BF16, kind="ExternalInput").ap()
    y_d = nc.dram_tensor("y", [S_tok, D], F32, kind="ExternalOutput").ap()
    qT_d = nc.dram_tensor("qT_s", [NT, 128, D], BF16, kind=KIND).ap()
    xqT_d = nc.dram_tensor("xqT_s", [NT, 128, D], BF16, kind=KIND).ap()
    v_dd = nc.dram_tensor("v_s", [NT, 128, D], BF16, kind=KIND).ap()
    s_dd = nc.dram_tensor("s_s", [NT, 128, D], BF16, kind=KIND).ap()
    wgT_d = nc.dram_tensor("wgT_s", [NIC, 128, NKC * 128], BF16, kind=KIND).ap()
    wuT_d = nc.dram_tensor("wuT_s", [NIC, 128, NKC * 128], BF16, kind=KIND).ap()
    wdT_d = nc.dram_tensor("wdT_s", [NKC, 128, NIC * 128], BF16, kind=KIND).ap()

    NW = 2 * NIC + NKC
    wt_of = [[] for _ in range(NT)]
    for t in range(NW):
        wt_of[min(NT - 1, t * NT // NW)].append(t)

    def wt_src(t):
        if t < NIC:
            return wg_d[t * 128:(t + 1) * 128, :], D, wgT_d[t], NKC
        if t < 2 * NIC:
            tt = t - NIC
            return wu_d[tt * 128:(tt + 1) * 128, :], D, wuT_d[tt], NKC
        tt = t - 2 * NIC
        return wd_d[tt * 128:(tt + 1) * 128, :], FI, wdT_d[tt], NIC

    CH = [1 if wt_src(t)[1] == D else 2 for t in range(NW)]
    NBAT = [(wt_src(t)[3] + 7) // 8 for t in range(NW)]
    CU = np.cumsum([0] + CH).tolist()
    CB = np.cumsum([0] + NBAT).tolist()
    CB_tile = []
    acc = 0
    for i in range(NT):
        for t in wt_of[i]:
            acc += NBAT[t]
        CB_tile.append(acc)

    st_all = contextlib.ExitStack()
    with st_all:
        E = st_all.enter_context
        SEM = {}
        for name in ("SX", "SWQ", "SWT", "SWW", "SPV", "SPQ", "SPXQ",
                     "c_xread", "c_ssq", "c_var", "c_std", "c_xnbf", "c_usgn",
                     "c_u", "c_xnT", "c_xq", "c_kqv", "c_ktev", "c_qtev",
                     "c_xqtev", "c_xntr", "c_qkv", "c_ktr", "c_qtr", "c_xqtr",
                     "w_u", "w_usgn", "w_q", "w_tr", "w_tev",
                     "SQR", "SVR", "SWO", "SSW",
                     "p_Sq", "p_Sev", "p_max", "p_bias", "p_exp", "p_pttr",
                     "p_ptev", "p_pv", "p_oev", "p_ottr", "p_otev", "p_smm",
                     "p_sev",
                     "SWR", "SXQ", "SSR", "SX2", "SY",
                     "f_gu", "f_sil", "f_hmul", "f_hu", "f_husgn", "f_hq",
                     "f_httr", "f_htev", "f_down", "f_gsil", "f_comb"):
            SEM[name] = E(nc.semaphore(name))
        S = SEM
        VSY = E(nc.semaphore("VSY"))
        vsy_n = [0]

        def mk_vbar(v):
            def vbar(ins):
                vsy_n[0] += 1
                ins.then_inc(VSY, 1)
                v.wait_ge(VSY, vsy_n[0])
            return vbar

        ident = E(nc.sbuf_tensor("ident_sb", [128, 128], BF16))
        stats = E(nc.sbuf_tensor("stats", [128, 16], F32))
        st_kt = contextlib.ExitStack()
        kT_sb = st_kt.enter_context(
            nc.sbuf_tensor("kT_sb", [128, NKC * S_tok], BF16, side="right"))
        blk = E(nc.Block())

        @blk.sync
        def _(sy):
            sy.dma_start(out=ident[:], in_=id_d).then_inc(S["SX"], 16)

        with contextlib.ExitStack() as st1:
            E1 = st1.enter_context
            wqkv = E1(nc.sbuf_tensor("wqkv", [128, NKC * 3072], BF16))
            xbuf = [E1(nc.sbuf_tensor(f"xbuf{j}", [128, D], F32)) for j in range(3)]
            xc = E1(nc.sbuf_tensor("xc", [128, D], F32))
            xn = E1(nc.sbuf_tensor("xn", [128, D], F32))
            xn_bf = E1(nc.sbuf_tensor("xn_bf", [128, D], BF16))
            xnT = E1(nc.sbuf_tensor("xnT", [128, D], BF16))
            QF = 1408
            u_b = E1(nc.sbuf_tensor("u_b", [128, QF], F32))
            usq = E1(nc.sbuf_tensor("usq", [128, QF], F32))
            sgn = E1(nc.sbuf_tensor("sgn", [128, QF], F32))
            cA = E1(nc.sbuf_tensor("cA", [128, QF], F32))
            cB = E1(nc.sbuf_tensor("cB", [128, QF], F32))
            msr = E1(nc.sbuf_tensor("msr", [128, 3 * 96], F32))
            m_s, s_s, r_s = msr[:, 0:96], msr[:, 96:192], msr[:, 192:288]
            xq_bf = E1(nc.sbuf_tensor("xq_bf", [128, D], BF16))
            q_bf = E1(nc.sbuf_tensor("q_bf", [128, D], BF16))
            k_bf = E1(nc.sbuf_tensor("k_bf", [128, D], BF16))
            v_bf = [E1(nc.sbuf_tensor(f"v_bf{j}", [128, D], BF16)) for j in range(2)]
            qT_bf = [E1(nc.sbuf_tensor(f"qT_bf{j}", [128, D], BF16)) for j in range(2)]
            xqT_bf = [E1(nc.sbuf_tensor(f"xqT_bf{j}", [128, D], BF16)) for j in range(2)]
            wstage = E1(nc.sbuf_tensor("wstage", [128, FI], F32))
            wq_bf = E1(nc.sbuf_tensor("wq_bf", [128, FI], BF16))
            wT_st = E1(nc.sbuf_tensor("wT_st", [128, FI], BF16))
            ps_qkv = [E1(nc.psum_tensor(f"ps_qkv{j}", [128, 512], F32))
                      for j in range(6)]
            ps_t1 = E1(nc.psum_tensor("ps_t1", [128, 1024], BF16))
            ps_t2 = E1(nc.psum_tensor("ps_t2", [128, 1024], BF16))

            @blk.gpsimd
            def _(g):
                for kc in range(NKC):
                    for wi, wsrc in enumerate((wq_d, wk_d, wv_d)):
                        c0 = kc * 3072 + wi * 1024
                        g.dma_start(out=wqkv[:, c0:c0 + 1024],
                                    in_=wsrc[kc * 128:(kc + 1) * 128, :]
                                    ).then_inc(S["SWQ"], 16)

            for i in range(NT):
                wtiles = wt_of[i]

                @blk.sync
                def _(sy, i=i):
                    if i >= 3:
                        sy.wait_ge(S["c_xread"], i - 2)
                    sy.dma_start(out=xbuf[i % 3][:],
                                 in_=x_d[i * 128:(i + 1) * 128, :]
                                 ).then_inc(S["SX"], 16)

                @blk.vector
                def _(v, i=i):
                    if serial & 1 and i > 0:
                        v.wait_ge(S["c_xqtev"], i)
                        if CB_tile[i - 1] > 0:
                            v.wait_ge(S["w_tev"], CB_tile[i - 1])
                    v.wait_ge(S["SX"], 16 * (i + 2))
                    xb = xbuf[i % 3]
                    vbar = mk_vbar(v)
                    vbar(v.tensor_reduce(out=stats[:, 0:1], in_=xb[:], axis=AX,
                                         op=A.add, negate=True))
                    vbar(v.tensor_scalar_mul(out=stats[:, 1:2],
                                             in0=stats[:, 0:1], scalar1=1.0 / D))
                    v.tensor_scalar_add(out=xc[:], in0=xb[:],
                                        scalar1=stats[:, 1:2]
                                        ).then_inc(S["c_xread"], 1)

                @blk.vector
                def _(v, i=i):
                    vbar = mk_vbar(v)
                    vbar(v.scalar_tensor_tensor(out=u_b[:, :D], in0=xc[:],
                                                scalar=1.0, in1=xc[:],
                                                op0=A.mult, op1=A.mult,
                                                accum_out=stats[:, 12:13]))
                    v.tensor_scalar(out=stats[:, 2:3], in0=stats[:, 12:13],
                                    scalar1=1.0 / D, scalar2=1e-5,
                                    op0=A.mult, op1=A.add).then_inc(S["c_var"], 1)

                @blk.scalar
                def _(s, i=i):
                    s.wait_ge(S["c_var"], i + 1)
                    s.activation(out=stats[:, 3:4], in_=stats[:, 2:3],
                                 func=AF.Sqrt).then_inc(S["c_std"], 1)
                    s.wait_ge(S["c_u"], i + 1)
                    s.activation(out=xn_bf[:], in_=xc[:], func=AF.Identity,
                                 scale=stats[:, 4:5]).then_inc(S["c_xnbf"], 1)
                    s.wait_ge(S["c_kqv"], i + 1)
                    s.dma_start(out=v_dd[i], in_=v_bf[i % 2][:]).then_inc(S["SPV"], 16)
                    s.wait_ge(S["c_qtev"], i + 1)
                    s.dma_start(out=qT_d[i], in_=qT_bf[i % 2][:]).then_inc(S["SPQ"], 16)
                    s.wait_ge(S["c_xqtev"], i + 1)
                    s.dma_start(out=xqT_d[i],
                                in_=xqT_bf[i % 2][:]).then_inc(S["SPXQ"], 16)

                @blk.vector
                def _(v, i=i):
                    v.wait_ge(S["c_std"], i + 1)
                    vbar = mk_vbar(v)
                    vbar(v.reciprocal(out=stats[:, 4:5], in_=stats[:, 3:4]))
                    v.tensor_scalar_mul(out=xn[:], in0=xc[:],
                                        scalar1=stats[:, 4:5])
                    x3 = xn[:].rearrange("p (nb b) -> p nb b", b=16)
                    _quant_head(v, x3, m_s, s_s, r_s, 64, u_b,
                                vbar).then_inc(S["c_u"], 1)

                @blk.tensor
                def _(t_, i=i):
                    t_.wait_ge(S["c_xnbf"], i + 1)
                    if i > 0:
                        t_.wait_ge(S["c_qtev"], i)
                        if CB_tile[i - 1] > 0:
                            t_.wait_ge(S["w_tev"], CB_tile[i - 1])
                    for kc in range(NKC):
                        ins = t_.transpose(out=ps_t1[:, kc * 128:(kc + 1) * 128],
                                           in_=xn_bf[:, kc * 128:(kc + 1) * 128],
                                           identity=ident[:])
                    ins.then_inc(S["c_xntr"], 1)

                @blk.vector
                def _(v, i=i):
                    v.wait_ge(S["c_u"], i + 1)
                    _quant_chain(v, u_b, s_s, 64,
                                 xq_bf[:].rearrange("p (nb b) -> p nb b", b=16),
                                 cA, cB, usq, sgn).then_inc(S["c_xq"], 1)
                    v.wait_ge(S["c_xntr"], i + 1)
                    v.tensor_copy(out=xnT[:], in_=ps_t1[:]).then_inc(S["c_xnT"], 1)

                @blk.tensor
                def _(t_, i=i):
                    if i == 0:
                        t_.wait_ge(S["SWQ"], 16 * NKC * 3)
                    t_.wait_ge(S["c_xnT"], i + 1)
                    if i > 0:
                        t_.wait_ge(S["c_kqv"], i)
                    for kc in range(NKC):
                        for ns in range(6):
                            c0 = kc * 3072 + ns * 512
                            ins = t_.matmul(out=ps_qkv[ns][:],
                                            lhsT=xnT[:, kc * 128:(kc + 1) * 128],
                                            rhs=wqkv[:, c0:c0 + 512],
                                            start=(kc == 0), stop=(kc == NKC - 1))
                    ins.then_inc(S["c_qkv"], 1)

                @blk.vector
                def _(v, i=i):
                    v.wait_ge(S["c_qkv"], i + 1)
                    if i >= 2:
                        v.wait_ge(S["SPV"], 16 * (i - 1))
                    v.tensor_copy(out=q_bf[:, 0:512], in_=ps_qkv[0][:])
                    v.tensor_copy(out=q_bf[:, 512:1024], in_=ps_qkv[1][:])
                    v.tensor_copy(out=k_bf[:, 0:512], in_=ps_qkv[2][:])
                    v.tensor_copy(out=k_bf[:, 512:1024], in_=ps_qkv[3][:])
                    v.tensor_copy(out=v_bf[i % 2][:, 0:512], in_=ps_qkv[4][:])
                    v.tensor_copy(out=v_bf[i % 2][:, 512:1024],
                                  in_=ps_qkv[5][:]).then_inc(S["c_kqv"], 1)

                @blk.tensor
                def _(t_, i=i):
                    t_.wait_ge(S["c_kqv"], i + 1)
                    if i > 0:
                        t_.wait_ge(S["c_xqtev"], i)
                    for kc in range(NKC):
                        ins = t_.transpose(out=ps_t2[:, kc * 128:(kc + 1) * 128],
                                           in_=k_bf[:, kc * 128:(kc + 1) * 128],
                                           identity=ident[:])
                    ins.then_inc(S["c_ktr"], 1)

                @blk.vector
                def _(v, i=i):
                    v.wait_ge(S["c_ktr"], i + 1)
                    kdst = kT_sb[:].rearrange("p (dc s) -> p dc s", s=S_tok)[
                        :, :, i * 128:(i + 1) * 128]
                    v.tensor_copy(out=kdst, in_=ps_t2[:].rearrange(
                        "p (dc j) -> p dc j", j=128)).then_inc(S["c_ktev"], 1)

                @blk.tensor
                def _(t_, i=i):
                    for kc in range(NKC):
                        ins = t_.transpose(out=ps_t1[:, kc * 128:(kc + 1) * 128],
                                           in_=q_bf[:, kc * 128:(kc + 1) * 128],
                                           identity=ident[:])
                    ins.then_inc(S["c_qtr"], 1)

                @blk.vector
                def _(v, i=i):
                    v.wait_ge(S["c_qtr"], i + 1)
                    if i >= 2:
                        v.wait_ge(S["SPQ"], 16 * (i - 1))
                    v.tensor_copy(out=qT_bf[i % 2][:],
                                  in_=ps_t1[:]).then_inc(S["c_qtev"], 1)

                @blk.tensor
                def _(t_, i=i):
                    t_.wait_ge(S["c_ktev"], i + 1)
                    t_.wait_ge(S["c_xq"], i + 1)
                    for kc in range(NKC):
                        ins = t_.transpose(out=ps_t2[:, kc * 128:(kc + 1) * 128],
                                           in_=xq_bf[:, kc * 128:(kc + 1) * 128],
                                           identity=ident[:])
                    ins.then_inc(S["c_xqtr"], 1)

                @blk.vector
                def _(v, i=i):
                    v.wait_ge(S["c_xqtr"], i + 1)
                    if i >= 2:
                        v.wait_ge(S["SPXQ"], 16 * (i - 1))
                    v.tensor_copy(out=xqT_bf[i % 2][:],
                                  in_=ps_t2[:]).then_inc(S["c_xqtev"], 1)

                for t in wtiles:
                    src, wcols, dstT, nblk = wt_src(t)
                    nch = CH[t]
                    cw = wcols // nch
                    nb = cw // 16

                    @blk.gpsimd
                    def _(g, t=t, src=src, wcols=wcols):
                        if t >= 1:
                            g.wait_ge(S["w_u"], CU[t])
                        g.dma_start(out=wstage[:, :wcols],
                                    in_=src).then_inc(S["SWT"], 16)

                    for c in range(nch):
                        @blk.vector
                        def _(v, t=t, c=c, cw=cw, nb=nb):
                            if c == 0:
                                v.wait_ge(S["SWT"], 16 * (t + 1))
                                if t >= 1:
                                    v.wait_ge(S["w_tr"], CB[t])
                            w3 = wstage[:, c * cw:(c + 1) * cw].rearrange(
                                "p (nb b) -> p nb b", b=16)
                            _quant_head(v, w3, m_s, s_s, r_s, nb, u_b,
                                        mk_vbar(v)).then_inc(S["w_u"], 1)

                        @blk.vector
                        def _(v, t=t, c=c, cw=cw, nb=nb):
                            v.wait_ge(S["w_u"], CU[t] + c + 1)
                            _quant_chain(
                                v, u_b, s_s, nb,
                                wq_bf[:, c * cw:(c + 1) * cw].rearrange(
                                    "p (nb b) -> p nb b", b=16),
                                cA, cB, usq, sgn).then_inc(S["w_q"], 1)

                    for b in range(NBAT[t]):
                        lo, hi = b * 8, min(nblk, b * 8 + 8)

                        @blk.tensor
                        def _(t_, t=t, b=b, lo=lo, hi=hi, i=i):
                            t_.wait_ge(S["w_q"], CU[t] + CH[t])
                            if b == 0 and t == wtiles[0]:
                                t_.wait_ge(S["c_qtev"], i + 1)
                            if CB[t] + b > 0:
                                t_.wait_ge(S["w_tev"], CB[t] + b)
                            for kk in range(lo, hi):
                                ins = t_.transpose(
                                    out=ps_t1[:, (kk - lo) * 128:(kk - lo + 1) * 128],
                                    in_=wq_bf[:, kk * 128:(kk + 1) * 128],
                                    identity=ident[:])
                            ins.then_inc(S["w_tr"], 1)

                        @blk.vector
                        def _(v, t=t, b=b, lo=lo, hi=hi):
                            v.wait_ge(S["w_tr"], CB[t] + b + 1)
                            if t >= 1 and b == 0:
                                v.wait_ge(S["SWW"], 16 * t)
                            v.tensor_copy(
                                out=wT_st[:, lo * 128:hi * 128],
                                in_=ps_t1[:, 0:(hi - lo) * 128]
                            ).then_inc(S["w_tev"], 1)

                    @blk.gpsimd
                    def _(g, t=t, dstT=dstT, nblk=nblk):
                        g.wait_ge(S["w_tev"], CB[t] + NBAT[t])
                        g.dma_start(out=dstT[:, :],
                                    in_=wT_st[:, :nblk * 128]).then_inc(S["SWW"], 16)

        with contextlib.ExitStack() as st2:
            E2 = st2.enter_context
            v_sb = E2(nc.sbuf_tensor("v_sb", [128, S_tok * NKC], BF16))
            wo_sb = E2(nc.sbuf_tensor("wo_sb", [128, NKC * 1024], BF16))
            S_row = E2(nc.sbuf_tensor("S_row", [128, S_tok], F32))
            P_bf = [E2(nc.sbuf_tensor(f"P_bf{j}", [128, S_tok], BF16))
                    for j in range(2)]
            PT_sb = E2(nc.sbuf_tensor("PT_sb", [128, S_tok], BF16))
            qT_t = [E2(nc.sbuf_tensor(f"qT_t{j}", [128, D], BF16)) for j in range(2)]
            o_bf = E2(nc.sbuf_tensor("o_bf", [128, D], BF16))
            oT_sb = E2(nc.sbuf_tensor("oT_sb", [128, D], BF16))
            s_bf = [E2(nc.sbuf_tensor(f"s_bf{j}", [128, D], BF16)) for j in range(2)]
            ps_s = [E2(nc.psum_tensor(f"ps_s{j}", [128, 1024], F32)) for j in range(2)]
            ps_pt = E2(nc.psum_tensor("ps_pt", [128, 1024], BF16))
            ps_o = E2(nc.psum_tensor("ps_o", [128, 1024], F32))
            ps_ot = E2(nc.psum_tensor("ps_ot", [128, 1024], BF16))
            NQ = S_tok // 1024
            NJC = S_tok // 128
            NPB = NJC // 8

            CBT = CB[NW]

            @blk.gpsimd
            def _(g):
                g.wait_ge(S["c_xqtev"], NT)
                if CBT:
                    g.wait_ge(S["w_tev"], CBT)
                    g.wait_ge(S["w_tr"], CBT)
                g.wait_ge(S["c_xqtr"], NT)
                g.wait_ge(S["SPV"], 16 * NT)
                g.wait_ge(S["SPQ"], 16 * NT)
                g.wait_ge(S["SPXQ"], 16 * NT)
                for kc in range(NKC):
                    g.dma_start(out=wo_sb[:, kc * 1024:(kc + 1) * 1024],
                                in_=wo_d[kc * 128:(kc + 1) * 128, :]
                                ).then_inc(S["SWO"], 16)

            @blk.sync
            def _(sy):
                sy.wait_ge(S["SPV"], 16 * NT)
                sy.wait_ge(S["SPQ"], 16 * NT)
                sy.wait_ge(S["SPXQ"], 16 * NT)
                sy.wait_ge(S["c_xqtev"], NT)
                if CBT:
                    sy.wait_ge(S["w_tev"], CBT)
                    sy.wait_ge(S["w_tr"], CBT)
                sy.wait_ge(S["c_xqtr"], NT)
                for j in range(NT):
                    sy.dma_start(
                        out=v_sb[:].rearrange("p (j c) -> p j c", c=1024)[:, j, :],
                        in_=v_dd[j]).then_inc(S["SVR"], 16)

            for i in range(NT + 1):
                if i < NT:
                    @blk.sync
                    def _(sy, i=i):
                        sy.wait_ge(S["SPQ"], 16 * (i + 1))
                        if i >= 2:
                            sy.wait_ge(S["p_smm"], i - 1)
                        sy.dma_start(out=qT_t[i % 2][:],
                                     in_=qT_d[i]).then_inc(S["SQR"], 16)

                    for qq in range(NQ):
                        @blk.tensor
                        def _(t_, i=i, qq=qq):
                            if serial & 2 and qq == 0 and i > 0:
                                t_.wait_ge(S["p_sev"], i)
                            if i == 0 and qq == 0:
                                t_.wait_ge(S["SVR"], 16 * NT)
                                t_.wait_ge(S["c_ktev"], NT)
                                t_.wait_ge(S["c_xqtev"], NT)
                                if CBT:
                                    t_.wait_ge(S["w_tev"], CBT)
                            if qq == 0:
                                t_.wait_ge(S["SQR"], 16 * (i + 1))
                            prev = NQ * i + qq - (2 if NQ >= 2 else 1) + 1
                            if prev > 0:
                                t_.wait_ge(S["p_Sev"], prev)
                                t_.wait_ge(S["p_max"], prev)
                            for dc in range(NKC):
                                for sl in range(2):
                                    col = dc * S_tok + qq * 1024 + sl * 512
                                    ins = t_.matmul(
                                        out=ps_s[qq % 2][:, sl * 512:(sl + 1) * 512],
                                        lhsT=qT_t[i % 2][:, dc * 128:(dc + 1) * 128],
                                        rhs=kT_sb[:, col:col + 512],
                                        start=(dc == 0), stop=(dc == NKC - 1))
                            ins.then_inc(S["p_Sq"], 1)

                        @blk.scalar
                        def _(s, i=i, qq=qq):
                            s.wait_ge(S["p_Sq"], NQ * i + qq + 1)
                            s.activation(out=S_row[:, qq * 1024:(qq + 1) * 1024],
                                         in_=ps_s[qq % 2][:],
                                         func=AF.Identity).then_inc(S["p_Sev"], 1)

                        @blk.vector
                        def _(v, i=i, qq=qq):
                            v.wait_ge(S["p_Sq"], NQ * i + qq + 1)
                            v.tensor_reduce(out=stats[:, 7 + qq:8 + qq],
                                            in_=ps_s[qq % 2][:], axis=AX, op=A.max,
                                            negate=True).then_inc(S["p_max"], 1)

                    @blk.vector
                    def _(v, i=i):
                        v.wait_ge(S["p_max"], NQ * (i + 1))
                        vbar = mk_vbar(v)
                        if NQ >= 2:
                            vbar(v.tensor_tensor(out=stats[:, 7:8],
                                                 in0=stats[:, 7:8],
                                                 in1=stats[:, 8:9], op=A.min))
                        if NQ >= 4:
                            vbar(v.tensor_tensor(out=stats[:, 9:10],
                                                 in0=stats[:, 9:10],
                                                 in1=stats[:, 10:11], op=A.min))
                            vbar(v.tensor_tensor(out=stats[:, 7:8],
                                                 in0=stats[:, 7:8],
                                                 in1=stats[:, 9:10], op=A.min))
                        bcol = 11 + 2 * (i % 2)
                        v.tensor_scalar_mul(out=stats[:, bcol:bcol + 1],
                                            in0=stats[:, 7:8],
                                            scalar1=8.0).then_inc(S["p_bias"], 1)

                    @blk.scalar
                    def _(s, i=i):
                        s.wait_ge(S["p_bias"], i + 1)
                        if i >= 2:
                            s.wait_ge(S["p_pttr"], NPB * (i - 1))
                        bcol = 11 + 2 * (i % 2)
                        s.activation(out=P_bf[i % 2][:], in_=S_row[:], func=AF.Exp,
                                     bias=stats[:, bcol:bcol + 1], scale=8.0,
                                     accum_out=stats[:, 5:6]).then_inc(S["p_exp"], 1)

                if i >= 1:
                    j = i - 1
                    for b in range(NPB):
                        @blk.tensor
                        def _(t_, j=j, b=b):
                            if b == 0:
                                t_.wait_ge(S["p_exp"], j + 1)
                            prev = NPB * j + b
                            if prev > 0:
                                t_.wait_ge(S["p_ptev"], prev)
                            for k in range(8):
                                jc = b * 8 + k
                                ins = t_.transpose(
                                    out=ps_pt[:, k * 128:(k + 1) * 128],
                                    in_=P_bf[j % 2][:, jc * 128:(jc + 1) * 128],
                                    identity=ident[:])
                            ins.then_inc(S["p_pttr"], 1)

                        @blk.vector
                        def _(v, j=j, b=b):
                            v.wait_ge(S["p_pttr"], NPB * j + b + 1)
                            v.tensor_copy(out=PT_sb[:, b * 1024:(b + 1) * 1024],
                                          in_=ps_pt[:]).then_inc(S["p_ptev"], 1)

                    @blk.tensor
                    def _(t_, j=j):
                        if j > 0:
                            t_.wait_ge(S["p_sev"], j)
                        for b in range(NPB):
                            t_.wait_ge(S["p_ptev"], NPB * j + b + 1)
                            for k in range(8):
                                jc = b * 8 + k
                                for dh in range(2):
                                    col = jc * 1024 + dh * 512
                                    ins = t_.matmul(
                                        out=ps_o[:, dh * 512:(dh + 1) * 512],
                                        lhsT=PT_sb[:, jc * 128:(jc + 1) * 128],
                                        rhs=v_sb[:, col:col + 512],
                                        start=(jc == 0), stop=(jc == NJC - 1))
                        ins.then_inc(S["p_pv"], 1)

                    @blk.vector
                    def _(v, j=j):
                        v.wait_ge(S["p_exp"], j + 1)
                        vbar = mk_vbar(v)
                        vbar(v.reciprocal(out=stats[:, 6:7], in_=stats[:, 5:6]))
                        v.wait_ge(S["p_pv"], j + 1)
                        v.tensor_scalar_mul(
                            out=o_bf[:], in0=ps_o[:],
                            scalar1=stats[:, 6:7]).then_inc(S["p_oev"], 1)

                    @blk.tensor
                    def _(t_, j=j):
                        t_.wait_ge(S["p_oev"], j + 1)
                        if j > 0:
                            t_.wait_ge(S["p_otev"], j)
                        for kc in range(NKC):
                            ins = t_.transpose(out=ps_ot[:, kc * 128:(kc + 1) * 128],
                                               in_=o_bf[:, kc * 128:(kc + 1) * 128],
                                               identity=ident[:])
                        ins.then_inc(S["p_ottr"], 1)

                    @blk.vector
                    def _(v, j=j):
                        v.wait_ge(S["p_ottr"], j + 1)
                        v.tensor_copy(out=oT_sb[:],
                                      in_=ps_ot[:]).then_inc(S["p_otev"], 1)

                    @blk.tensor
                    def _(t_, j=j):
                        if j == 0:
                            t_.wait_ge(S["SWO"], 16 * NKC)
                        t_.wait_ge(S["p_otev"], j + 1)
                        for dc in range(NKC):
                            for sl in range(2):
                                col = dc * 1024 + sl * 512
                                ins = t_.matmul(
                                    out=ps_o[:, sl * 512:(sl + 1) * 512],
                                    lhsT=oT_sb[:, dc * 128:(dc + 1) * 128],
                                    rhs=wo_sb[:, col:col + 512],
                                    start=(dc == 0), stop=(dc == NKC - 1))
                        ins.then_inc(S["p_smm"], 1)

                    @blk.vector
                    def _(v, j=j):
                        v.wait_ge(S["p_smm"], j + 1)
                        if j >= 2:
                            v.wait_ge(S["SSW"], 16 * (j - 1))
                        v.tensor_copy(out=s_bf[j % 2][:],
                                      in_=ps_o[:]).then_inc(S["p_sev"], 1)

                    @blk.scalar
                    def _(s, j=j):
                        s.wait_ge(S["p_sev"], j + 1)
                        s.dma_start(out=s_dd[j],
                                    in_=s_bf[j % 2][:]).then_inc(S["SSW"], 16)

        st_kt.close()

        with contextlib.ExitStack() as st3:
            E3 = st3.enter_context
            wgT_sb = E3(nc.sbuf_tensor("wgT_sb", [128, NIC * 1024], BF16))
            wuT_sb = E3(nc.sbuf_tensor("wuT_sb", [128, NIC * 1024], BF16))
            wdT_sb = E3(nc.sbuf_tensor("wdT_sb", [128, NKC * FI], BF16))
            xqT_t = [E3(nc.sbuf_tensor(f"xqT_t{j}", [128, D], BF16)) for j in range(2)]
            s_t = [E3(nc.sbuf_tensor(f"s_t{j}", [128, D], BF16)) for j in range(2)]
            x2 = [E3(nc.sbuf_tensor(f"x2_{j}", [128, D], F32)) for j in range(2)]
            h_f = E3(nc.sbuf_tensor("h_f", [128, FI], F32))
            hq_bf = [E3(nc.sbuf_tensor(f"hq_bf{j}", [128, FI], BF16))
                     for j in range(2)]
            hqT_sb = E3(nc.sbuf_tensor("hqT_sb", [128, FI], BF16))
            hg_t = E3(nc.sbuf_tensor("hg_t", [128, 512], F32))
            g_f = E3(nc.sbuf_tensor("g_f", [128, D], F32))
            out_sb = [E3(nc.sbuf_tensor(f"out_sb{j}", [128, D], F32))
                      for j in range(2)]
            QF3 = 704
            NHC = FI // QF3
            u3_b = E3(nc.sbuf_tensor("u3_b", [128, QF3], F32))
            usq3 = E3(nc.sbuf_tensor("usq3", [128, QF3], F32))
            sgn3 = E3(nc.sbuf_tensor("sgn3", [128, QF3], F32))
            cA3 = E3(nc.sbuf_tensor("cA3", [128, QF3], F32))
            cB3 = E3(nc.sbuf_tensor("cB3", [128, QF3], F32))
            msr3 = E3(nc.sbuf_tensor("msr3", [128, 3 * 48], F32))
            m3_s, s3_s, r3_s = msr3[:, 0:48], msr3[:, 48:96], msr3[:, 96:144]
            ps_g = [E3(nc.psum_tensor(f"ps_g{j}", [128, 512], F32)) for j in range(4)]
            ps_ht = E3(nc.psum_tensor("ps_ht", [128, 1024], BF16))
            ps_d = E3(nc.psum_tensor("ps_d", [128, 1024], F32))

            nwin = []
            off = 0
            while off < FI:
                w = min(512, FI - off)
                nwin.append((off, w))
                off += w
            NCW = len(nwin)
            NB3 = (NIC + 7) // 8
            wgT3 = wgT_sb[:].rearrange("p (nt kc j) -> p nt kc j", kc=NKC, j=128)
            wuT3 = wuT_sb[:].rearrange("p (nt kc j) -> p nt kc j", kc=NKC, j=128)
            wdT3 = wdT_sb[:].rearrange("p (dt ic j) -> p dt ic j", ic=NIC, j=128)
            hq_need = [min(NHC, -(-((b * 8 + 8) * 128) // QF3)) for b in range(NB3)]

            @blk.gpsimd
            def _(g):
                g.wait_ge(S["SWW"], 16 * NW)
                g.wait_ge(S["p_smm"], NT)
                g.wait_ge(S["p_sev"], NT)
                g.wait_ge(S["SSW"], 16 * NT)
                for t in range(NIC):
                    g.dma_start(out=wgT_sb[:, t * 1024:(t + 1) * 1024],
                                in_=wgT_d[t]).then_inc(S["SWR"], 16)
                for t in range(NIC):
                    g.dma_start(out=wuT_sb[:, t * 1024:(t + 1) * 1024],
                                in_=wuT_d[t]).then_inc(S["SWR"], 16)
                for t in range(NKC):
                    g.dma_start(out=wdT_sb[:, t * FI:(t + 1) * FI],
                                in_=wdT_d[t]).then_inc(S["SWR"], 16)

            for i in range(NT + 1):
                if i >= 1:
                    j = i - 1
                    for b in range(NB3):
                        lo, hi = b * 8, min(NIC, b * 8 + 8)

                        @blk.tensor
                        def _(t_, j=j, b=b, lo=lo, hi=hi):
                            t_.wait_ge(S["f_hq"], NHC * j + hq_need[b])
                            prev = NB3 * j + b
                            if prev > 0:
                                t_.wait_ge(S["f_htev"], prev)
                            for kk in range(lo, hi):
                                ins = t_.transpose(
                                    out=ps_ht[:, (kk - lo) * 128:(kk - lo + 1) * 128],
                                    in_=hq_bf[j % 2][:, kk * 128:(kk + 1) * 128],
                                    identity=ident[:])
                            ins.then_inc(S["f_httr"], 1)

                        @blk.vector
                        def _(v, j=j, b=b, lo=lo, hi=hi):
                            v.wait_ge(S["f_httr"], NB3 * j + b + 1)
                            v.tensor_copy(out=hqT_sb[:, lo * 128:hi * 128],
                                          in_=ps_ht[:, 0:(hi - lo) * 128]
                                          ).then_inc(S["f_htev"], 1)

                    @blk.tensor
                    def _(t_, j=j):
                        t_.wait_ge(S["f_htev"], NB3 * (j + 1))
                        if j > 0:
                            t_.wait_ge(S["f_gsil"], j)
                        for ic in range(NIC):
                            for sl in range(2):
                                ins = t_.matmul(
                                    out=ps_d[:, sl * 512:(sl + 1) * 512],
                                    lhsT=hqT_sb[:, ic * 128:(ic + 1) * 128],
                                    rhs=wdT3[:, sl * 4:sl * 4 + 4, ic, :],
                                    start=(ic == 0), stop=(ic == NIC - 1))
                        ins.then_inc(S["f_down"], 1)

                    @blk.scalar
                    def _(s, j=j):
                        s.wait_ge(S["f_down"], j + 1)
                        s.activation(out=g_f[:], in_=ps_d[:],
                                     func=AF.Silu).then_inc(S["f_gsil"], 1)

                if i >= 1:
                    j = i - 1
                    @blk.vector
                    def _(v, j=j):
                        v.wait_ge(S["f_gsil"], j + 1)
                        v.wait_ge(S["SSR"], 16 * (j + 1))
                        v.wait_ge(S["SX2"], 16 * (j + 1))
                        if j >= 2:
                            v.wait_ge(S["SY"], 16 * (j - 1))
                        v.tensor_tensor(out=g_f[:], in0=g_f[:], in1=s_t[j % 2][:],
                                        op=A.mult)
                        v.tensor_tensor(out=out_sb[j % 2][:], in0=g_f[:],
                                        in1=x2[j % 2][:], op=A.add
                                        ).then_inc(S["f_comb"], 1)

                    @blk.scalar
                    def _(s, j=j):
                        s.wait_ge(S["f_comb"], j + 1)
                        s.dma_start(out=y_d[j * 128:(j + 1) * 128, :],
                                    in_=out_sb[j % 2][:]).then_inc(S["SY"], 16)

                if i < NT:
                    @blk.sync
                    def _(sy, i=i):
                        if i == 0:
                            sy.wait_ge(S["p_smm"], NT)
                            sy.wait_ge(S["p_sev"], NT)
                        sy.wait_ge(S["SPXQ"], 16 * (i + 1))
                        if i >= 2:
                            sy.wait_ge(S["f_gu"], NCW * (i - 1))
                        sy.dma_start(out=xqT_t[i % 2][:],
                                     in_=xqT_d[i]).then_inc(S["SXQ"], 16)
                        sy.wait_ge(S["SSW"], 16 * (i + 1))
                        if i >= 2:
                            sy.wait_ge(S["f_comb"], i - 1)
                        sy.dma_start(out=s_t[i % 2][:],
                                     in_=s_dd[i]).then_inc(S["SSR"], 16)
                        sy.dma_start(out=x2[i % 2][:],
                                     in_=x_d[i * 128:(i + 1) * 128, :]
                                     ).then_inc(S["SX2"], 16)

                    for ci, (noff, nw) in enumerate(nwin):
                        pair = ci % 2

                        @blk.tensor
                        def _(t_, i=i, ci=ci, noff=noff, nw=nw, pair=pair):
                            if serial & 4 and ci == 0 and i > 0:
                                t_.wait_ge(S["f_comb"], i)
                            if i == 0 and ci == 0:
                                t_.wait_ge(S["SWR"], 16 * (2 * NIC + NKC))
                                t_.wait_ge(S["SSW"], 16 * NT)
                                t_.wait_ge(S["p_sev"], NT)
                            if ci == 0:
                                t_.wait_ge(S["SXQ"], 16 * (i + 1))
                            prev = NCW * i + ci - 1
                            if prev > 0:
                                t_.wait_ge(S["f_hmul"], prev)
                                t_.wait_ge(S["f_sil"], prev)
                            nt0, ntn = noff // 128, nw // 128
                            for kc in range(NKC):
                                t_.matmul(out=ps_g[2 * pair][:, :nw],
                                          lhsT=xqT_t[i % 2][:, kc * 128:(kc + 1) * 128],
                                          rhs=wgT3[:, nt0:nt0 + ntn, kc, :],
                                          start=(kc == 0), stop=(kc == NKC - 1))
                                ins = t_.matmul(
                                    out=ps_g[2 * pair + 1][:, :nw],
                                    lhsT=xqT_t[i % 2][:, kc * 128:(kc + 1) * 128],
                                    rhs=wuT3[:, nt0:nt0 + ntn, kc, :],
                                    start=(kc == 0), stop=(kc == NKC - 1))
                            ins.then_inc(S["f_gu"], 1)

                        @blk.scalar
                        def _(s, i=i, ci=ci, nw=nw, pair=pair):
                            s.wait_ge(S["f_gu"], NCW * i + ci + 1)
                            prev = NCW * i + ci
                            if prev > 0:
                                s.wait_ge(S["f_hmul"], prev)
                            s.activation(out=hg_t[:, :nw], in_=ps_g[2 * pair][:, :nw],
                                         func=AF.Silu).then_inc(S["f_sil"], 1)

                        @blk.vector
                        def _(v, i=i, ci=ci, noff=noff, nw=nw, pair=pair):
                            v.wait_ge(S["f_sil"], NCW * i + ci + 1)
                            v.tensor_tensor(out=h_f[:, noff:noff + nw],
                                            in0=hg_t[:, :nw],
                                            in1=ps_g[2 * pair + 1][:, :nw],
                                            op=A.mult).then_inc(S["f_hmul"], 1)

                        for c in range(NHC):
                            if -(-((c + 1) * QF3) // 512) == ci + 1:
                                @blk.vector
                                def _(v, i=i, c=c, ci=ci):
                                    if c == 0 and i >= 2:
                                        v.wait_ge(S["f_httr"], NB3 * (i - 1))
                                    v.wait_ge(S["f_hmul"], NCW * i + ci + 1)
                                    nb = QF3 // 16
                                    h3 = h_f[:, c * QF3:(c + 1) * QF3].rearrange(
                                        "p (nb b) -> p nb b", b=16)
                                    _quant_head(v, h3, m3_s, s3_s, r3_s, nb,
                                                u3_b, mk_vbar(v)
                                                ).then_inc(S["f_hu"], 1)
                                    v.wait_ge(S["f_hu"], NHC * i + c + 1)
                                    _quant_chain(
                                        v, u3_b, s3_s, nb,
                                        hq_bf[i % 2][:, c * QF3:(c + 1) * QF3
                                                     ].rearrange(
                                            "p (nb b) -> p nb b", b=16),
                                        cA3, cB3, usq3, sgn3
                                    ).then_inc(S["f_hq"], 1)


            @blk.scalar
            def _(s):
                s.wait_ge(S["SY"], 16 * NT)

    return nc


def kernel(L0_attn_x, L0_attn_Wq, L0_attn_Wk, L0_attn_Wv, L0_attn_Wo,
           L0_ffn_w_gate, L0_ffn_w_up, L0_ffn_w_down):
    x = np.ascontiguousarray(np.asarray(L0_attn_x, dtype=np.float32))
    B, S_tok, _ = x.shape
    ident = np.eye(128, dtype=ml_dtypes.bfloat16)
    weights = dict(
        wq=np.ascontiguousarray(np.asarray(L0_attn_Wq, dtype=np.float32)),
        wk=np.ascontiguousarray(np.asarray(L0_attn_Wk, dtype=np.float32)),
        wv=np.ascontiguousarray(np.asarray(L0_attn_Wv, dtype=np.float32)),
        wo=np.ascontiguousarray(np.asarray(L0_attn_Wo, dtype=np.float32)),
        wg=np.ascontiguousarray(np.asarray(L0_ffn_w_gate, dtype=np.float32)),
        wu=np.ascontiguousarray(np.asarray(L0_ffn_w_up, dtype=np.float32)),
        wd=np.ascontiguousarray(np.asarray(L0_ffn_w_down, dtype=np.float32)),
        ident=ident)
    nc = build(S_tok)
    in_maps = [dict(x=np.ascontiguousarray(x[c]), **weights) for c in range(B)]
    res = run_bass_kernel_spmd(nc, in_maps, core_ids=list(range(B)))
    return np.stack([np.asarray(r["y"], dtype=np.float32) for r in res.results],
                    axis=0)


# version 12
# speedup vs baseline: 1.0121x; 1.0121x over previous
import contextlib
import numpy as np
import ml_dtypes
import concourse.bass as bass
import concourse.mybir as mybir
from concourse.bass_utils import run_bass_kernel_spmd
from concourse.alu_op_type import AluOpType as A

F32 = mybir.dt.float32
BF16 = mybir.dt.bfloat16
AF = mybir.ActivationFunctionType
AX = mybir.AxisListType.X

D = 1024
FI = 2816
NKC = D // 128
NIC = FI // 128


def _quant_chain(v, u_b, s_small, nb, outcols, cA, cB, ab, sg2):
    F = nb * 16
    v.tensor_tensor(out=ab[:, :F], in0=u_b[:, :F], in1=u_b[:, :F], op=A.mult)
    v.tensor_scalar(out=sg2[:, :F], in0=u_b[:, :F], scalar1=0.0, scalar2=2.0,
                    op0=A.is_gt, op1=A.mult)
    v.tensor_single_scalar(out=cA[:, :F], in_=ab[:, :F], scalar=0.0625,
                           op=A.is_gt)
    for t2 in (0.5625, 1.5625, 3.0625):
        v.scalar_tensor_tensor(out=cA[:, :F], in0=ab[:, :F], scalar=t2,
                               in1=cA[:, :F], op0=A.is_gt, op1=A.add)
    v.tensor_scalar(out=cB[:, :F], in0=ab[:, :F], scalar1=25.0, scalar2=2.0,
                    op0=A.is_gt, op1=A.mult)
    for t2 in (12.25, 6.25):
        v.scalar_tensor_tensor(out=cB[:, :F], in0=ab[:, :F], scalar=t2,
                               in1=cB[:, :F], op0=A.is_gt, op1=A.add)
    v.scalar_tensor_tensor(out=cA[:, :F], in0=cA[:, :F], scalar=0.5,
                           in1=cB[:, :F], op0=A.mult, op1=A.add)
    v.scalar_tensor_tensor(out=cA[:, :F], in0=sg2[:, :F], scalar=-1.0,
                           in1=cA[:, :F], op0=A.add, op1=A.mult)
    sb3 = s_small[:, :nb].unsqueeze(2).broadcast_to([128, nb, 16])
    return v.tensor_tensor(
        out=outcols, in0=cA[:, :F].rearrange("p (nb b) -> p nb b", b=16),
        in1=sb3, op=A.mult)


def _quant_head(v, src3, m_s, s_s, r_s, nb, u_b, vbar):
    vbar(v.tensor_reduce(out=m_s[:, :nb], in_=src3, axis=AX, op=A.max,
                         apply_absolute_value=True))
    vbar(v.tensor_scalar(out=s_s[:, :nb], in0=m_s[:, :nb], scalar1=1.0 / 6.0,
                         scalar2=1e-12, op0=A.mult, op1=A.max))
    vbar(v.reciprocal(out=r_s[:, :nb], in_=s_s[:, :nb]))
    return v.tensor_tensor(
        out=u_b[:, :nb * 16].rearrange("p (nb b) -> p nb b", b=16), in0=src3,
        in1=r_s[:, :nb].unsqueeze(2).broadcast_to([128, nb, 16]), op=A.mult)


def build(S_tok=4096, dbg=False, serial=0):
    NT = S_tok // 128
    nc = bass.Bass("TRN2", target_bir_lowering=False, detect_race_conditions=False)
    KIND = "ExternalOutput" if dbg else "Internal"


    x_d = nc.dram_tensor("x", [S_tok, D], F32, kind="ExternalInput").ap()
    wq_d = nc.dram_tensor("wq", [D, D], F32, kind="ExternalInput").ap()
    wk_d = nc.dram_tensor("wk", [D, D], F32, kind="ExternalInput").ap()
    wv_d = nc.dram_tensor("wv", [D, D], F32, kind="ExternalInput").ap()
    wo_d = nc.dram_tensor("wo", [D, D], F32, kind="ExternalInput").ap()
    wg_d = nc.dram_tensor("wg", [FI, D], F32, kind="ExternalInput").ap()
    wu_d = nc.dram_tensor("wu", [FI, D], F32, kind="ExternalInput").ap()
    wd_d = nc.dram_tensor("wd", [D, FI], F32, kind="ExternalInput").ap()
    id_d = nc.dram_tensor("ident", [128, 128], BF16, kind="ExternalInput").ap()
    y_d = nc.dram_tensor("y", [S_tok, D], F32, kind="ExternalOutput").ap()
    qT_d = nc.dram_tensor("qT_s", [NT, 128, D], BF16, kind=KIND).ap()
    xqT_d = nc.dram_tensor("xqT_s", [NT, 128, D], BF16, kind=KIND).ap()
    v_dd = nc.dram_tensor("v_s", [NT, 128, D], BF16, kind=KIND).ap()
    s_dd = nc.dram_tensor("s_s", [NT, 128, D], BF16, kind=KIND).ap()
    wgT_d = nc.dram_tensor("wgT_s", [NIC, 128, NKC * 128], BF16, kind=KIND).ap()
    wuT_d = nc.dram_tensor("wuT_s", [NIC, 128, NKC * 128], BF16, kind=KIND).ap()
    wdT_d = nc.dram_tensor("wdT_s", [NKC, 128, NIC * 128], BF16, kind=KIND).ap()

    NW = 2 * NIC + NKC
    wt_of = [[] for _ in range(NT)]
    for t in range(NW):
        wt_of[min(NT - 1, t * NT // NW)].append(t)

    def wt_src(t):
        if t < NIC:
            return wg_d[t * 128:(t + 1) * 128, :], D, wgT_d[t], NKC
        if t < 2 * NIC:
            tt = t - NIC
            return wu_d[tt * 128:(tt + 1) * 128, :], D, wuT_d[tt], NKC
        tt = t - 2 * NIC
        return wd_d[tt * 128:(tt + 1) * 128, :], FI, wdT_d[tt], NIC

    CH = [1 if wt_src(t)[1] == D else 2 for t in range(NW)]
    NBAT = [(wt_src(t)[3] + 7) // 8 for t in range(NW)]
    CU = np.cumsum([0] + CH).tolist()
    CB = np.cumsum([0] + NBAT).tolist()
    CB_tile = []
    acc = 0
    for i in range(NT):
        for t in wt_of[i]:
            acc += NBAT[t]
        CB_tile.append(acc)

    st_all = contextlib.ExitStack()
    with st_all:
        E = st_all.enter_context
        SEM = {}
        for name in ("SX", "SWQ", "SWT", "SWW", "SPV", "SPQ", "SPXQ",
                     "c_xread", "c_ssq", "c_var", "c_std", "c_xnbf", "c_usgn",
                     "c_u", "c_xnT", "c_xq", "c_kqv", "c_ktev", "c_qtev",
                     "c_xqtev", "c_xntr", "c_qkv", "c_ktr", "c_qtr", "c_xqtr",
                     "w_u", "w_usgn", "w_q", "w_tr", "w_tev",
                     "SQR", "SVR", "SWO", "SSW",
                     "p_Sq", "p_Sev", "p_max", "p_bias", "p_exp", "p_pttr",
                     "p_ptev", "p_pv", "p_oev", "p_ottr", "p_otev", "p_smm",
                     "p_sev",
                     "SWR", "SXQ", "SSR", "SX2", "SY",
                     "f_gu", "f_sil", "f_hmul", "f_hu", "f_husgn", "f_hq",
                     "f_httr", "f_htev", "f_down", "f_gsil", "f_comb"):
            SEM[name] = E(nc.semaphore(name))
        S = SEM
        VSY = E(nc.semaphore("VSY"))
        vsy_n = [0]

        def mk_vbar(v):
            def vbar(ins):
                vsy_n[0] += 1
                ins.then_inc(VSY, 1)
                v.wait_ge(VSY, vsy_n[0])
            return vbar

        ident = E(nc.sbuf_tensor("ident_sb", [128, 128], BF16))
        stats = E(nc.sbuf_tensor("stats", [128, 16], F32))
        st_kt = contextlib.ExitStack()
        kT_sb = st_kt.enter_context(
            nc.sbuf_tensor("kT_sb", [128, NKC * S_tok], BF16, side="right"))
        blk = E(nc.Block())

        @blk.sync
        def _(sy):
            sy.dma_start(out=ident[:], in_=id_d).then_inc(S["SX"], 16)

        with contextlib.ExitStack() as st1:
            E1 = st1.enter_context
            wqkv = E1(nc.sbuf_tensor("wqkv", [128, NKC * 3072], BF16))
            xbuf = [E1(nc.sbuf_tensor(f"xbuf{j}", [128, D], F32)) for j in range(2)]
            xc = E1(nc.sbuf_tensor("xc", [128, D], F32))
            xn = E1(nc.sbuf_tensor("xn", [128, D], F32))
            xn_bf = E1(nc.sbuf_tensor("xn_bf", [128, D], BF16))
            xnT = E1(nc.sbuf_tensor("xnT", [128, D], BF16))
            QF = 1408
            u_b = E1(nc.sbuf_tensor("u_b", [128, QF], F32))
            usq = E1(nc.sbuf_tensor("usq", [128, QF], F32))
            sgn = E1(nc.sbuf_tensor("sgn", [128, QF], F32))
            cA = E1(nc.sbuf_tensor("cA", [128, QF], F32))
            cB = E1(nc.sbuf_tensor("cB", [128, QF], F32))
            msr = E1(nc.sbuf_tensor("msr", [128, 3 * 96], F32))
            m_s, s_s, r_s = msr[:, 0:96], msr[:, 96:192], msr[:, 192:288]
            xq_bf = E1(nc.sbuf_tensor("xq_bf", [128, D], BF16))
            q_bf = E1(nc.sbuf_tensor("q_bf", [128, D], BF16))
            k_bf = E1(nc.sbuf_tensor("k_bf", [128, D], BF16))
            v_bf = [E1(nc.sbuf_tensor(f"v_bf{j}", [128, D], BF16)) for j in range(2)]
            qT_bf = [E1(nc.sbuf_tensor(f"qT_bf{j}", [128, D], BF16)) for j in range(2)]
            xqT_bf = [E1(nc.sbuf_tensor(f"xqT_bf{j}", [128, D], BF16)) for j in range(2)]
            wstage = E1(nc.sbuf_tensor("wstage", [128, FI], F32))
            wq_bf = E1(nc.sbuf_tensor("wq_bf", [128, FI], BF16))
            wT_st = E1(nc.sbuf_tensor("wT_st", [128, FI], BF16))
            ps_qkv = [E1(nc.psum_tensor(f"ps_qkv{j}", [128, 512], F32))
                      for j in range(6)]
            ps_t1 = E1(nc.psum_tensor("ps_t1", [128, 1024], BF16))
            ps_t2 = E1(nc.psum_tensor("ps_t2", [128, 1024], BF16))

            @blk.gpsimd
            def _(g):
                for kc in range(NKC):
                    for wi, wsrc in enumerate((wq_d, wk_d, wv_d)):
                        c0 = kc * 3072 + wi * 1024
                        g.dma_start(out=wqkv[:, c0:c0 + 1024],
                                    in_=wsrc[kc * 128:(kc + 1) * 128, :]
                                    ).then_inc(S["SWQ"], 16)

            for i in range(NT):
                wtiles = wt_of[i]

                @blk.sync
                def _(sy, i=i):
                    if i >= 2:
                        sy.wait_ge(S["c_xread"], i - 1)
                    sy.dma_start(out=xbuf[i % 2][:],
                                 in_=x_d[i * 128:(i + 1) * 128, :]
                                 ).then_inc(S["SX"], 16)

                @blk.vector
                def _(v, i=i):
                    if serial & 1 and i > 0:
                        v.wait_ge(S["c_xqtev"], i)
                        if CB_tile[i - 1] > 0:
                            v.wait_ge(S["w_tev"], CB_tile[i - 1])
                    v.wait_ge(S["SX"], 16 * (i + 2))
                    xb = xbuf[i % 2]
                    vbar = mk_vbar(v)
                    vbar(v.tensor_reduce(out=stats[:, 0:1], in_=xb[:], axis=AX,
                                         op=A.add, negate=True))
                    vbar(v.tensor_scalar_mul(out=stats[:, 1:2],
                                             in0=stats[:, 0:1], scalar1=1.0 / D))
                    v.tensor_scalar_add(out=xc[:], in0=xb[:],
                                        scalar1=stats[:, 1:2]
                                        ).then_inc(S["c_xread"], 1)

                @blk.vector
                def _(v, i=i):
                    vbar = mk_vbar(v)
                    vbar(v.scalar_tensor_tensor(out=u_b[:, :D], in0=xc[:],
                                                scalar=1.0, in1=xc[:],
                                                op0=A.mult, op1=A.mult,
                                                accum_out=stats[:, 12:13]))
                    v.tensor_scalar(out=stats[:, 2:3], in0=stats[:, 12:13],
                                    scalar1=1.0 / D, scalar2=1e-5,
                                    op0=A.mult, op1=A.add).then_inc(S["c_var"], 1)

                @blk.scalar
                def _(s, i=i):
                    s.wait_ge(S["c_var"], i + 1)
                    s.activation(out=stats[:, 3:4], in_=stats[:, 2:3],
                                 func=AF.Sqrt).then_inc(S["c_std"], 1)
                    s.wait_ge(S["c_u"], i + 1)
                    s.activation(out=xn_bf[:], in_=xc[:], func=AF.Identity,
                                 scale=stats[:, 4:5]).then_inc(S["c_xnbf"], 1)
                    s.wait_ge(S["c_kqv"], i + 1)
                    s.dma_start(out=v_dd[i], in_=v_bf[i % 2][:]).then_inc(S["SPV"], 16)
                    s.wait_ge(S["c_qtev"], i + 1)
                    s.dma_start(out=qT_d[i], in_=qT_bf[i % 2][:]).then_inc(S["SPQ"], 16)
                    s.wait_ge(S["c_xqtev"], i + 1)
                    s.dma_start(out=xqT_d[i],
                                in_=xqT_bf[i % 2][:]).then_inc(S["SPXQ"], 16)

                @blk.vector
                def _(v, i=i):
                    v.wait_ge(S["c_std"], i + 1)
                    vbar = mk_vbar(v)
                    vbar(v.reciprocal(out=stats[:, 4:5], in_=stats[:, 3:4]))
                    v.tensor_scalar_mul(out=xn[:], in0=xc[:],
                                        scalar1=stats[:, 4:5])
                    x3 = xn[:].rearrange("p (nb b) -> p nb b", b=16)
                    _quant_head(v, x3, m_s, s_s, r_s, 64, u_b,
                                vbar).then_inc(S["c_u"], 1)

                @blk.tensor
                def _(t_, i=i):
                    t_.wait_ge(S["c_xnbf"], i + 1)
                    if i > 0:
                        t_.wait_ge(S["c_qtev"], i)
                        if CB_tile[i - 1] > 0:
                            t_.wait_ge(S["w_tev"], CB_tile[i - 1])
                    for kc in range(NKC):
                        ins = t_.transpose(out=ps_t1[:, kc * 128:(kc + 1) * 128],
                                           in_=xn_bf[:, kc * 128:(kc + 1) * 128],
                                           identity=ident[:])
                    ins.then_inc(S["c_xntr"], 1)

                @blk.vector
                def _(v, i=i):
                    v.wait_ge(S["c_xntr"], i + 1)
                    v.tensor_copy(out=xnT[:], in_=ps_t1[:]).then_inc(S["c_xnT"], 1)
                    v.wait_ge(S["c_u"], i + 1)
                    _quant_chain(v, u_b, s_s, 64,
                                 xq_bf[:].rearrange("p (nb b) -> p nb b", b=16),
                                 cA, cB, usq, sgn).then_inc(S["c_xq"], 1)

                @blk.tensor
                def _(t_, i=i):
                    if i == 0:
                        t_.wait_ge(S["SWQ"], 16 * NKC * 3)
                    t_.wait_ge(S["c_xnT"], i + 1)
                    if i > 0:
                        t_.wait_ge(S["c_kqv"], i)
                    for kc in range(NKC):
                        for ns in range(6):
                            c0 = kc * 3072 + ns * 512
                            ins = t_.matmul(out=ps_qkv[ns][:],
                                            lhsT=xnT[:, kc * 128:(kc + 1) * 128],
                                            rhs=wqkv[:, c0:c0 + 512],
                                            start=(kc == 0), stop=(kc == NKC - 1))
                    ins.then_inc(S["c_qkv"], 1)

                @blk.vector
                def _(v, i=i):
                    v.wait_ge(S["c_qkv"], i + 1)
                    if i >= 2:
                        v.wait_ge(S["SPV"], 16 * (i - 1))
                    v.tensor_copy(out=q_bf[:, 0:512], in_=ps_qkv[0][:])
                    v.tensor_copy(out=q_bf[:, 512:1024], in_=ps_qkv[1][:])
                    v.tensor_copy(out=k_bf[:, 0:512], in_=ps_qkv[2][:])
                    v.tensor_copy(out=k_bf[:, 512:1024], in_=ps_qkv[3][:])
                    v.tensor_copy(out=v_bf[i % 2][:, 0:512], in_=ps_qkv[4][:])
                    v.tensor_copy(out=v_bf[i % 2][:, 512:1024],
                                  in_=ps_qkv[5][:]).then_inc(S["c_kqv"], 1)

                @blk.tensor
                def _(t_, i=i):
                    t_.wait_ge(S["c_kqv"], i + 1)
                    if i > 0:
                        t_.wait_ge(S["c_xqtev"], i)
                    for kc in range(NKC):
                        ins = t_.transpose(out=ps_t2[:, kc * 128:(kc + 1) * 128],
                                           in_=k_bf[:, kc * 128:(kc + 1) * 128],
                                           identity=ident[:])
                    ins.then_inc(S["c_ktr"], 1)

                @blk.vector
                def _(v, i=i):
                    v.wait_ge(S["c_ktr"], i + 1)
                    kdst = kT_sb[:].rearrange("p (dc s) -> p dc s", s=S_tok)[
                        :, :, i * 128:(i + 1) * 128]
                    v.tensor_copy(out=kdst, in_=ps_t2[:].rearrange(
                        "p (dc j) -> p dc j", j=128)).then_inc(S["c_ktev"], 1)

                @blk.tensor
                def _(t_, i=i):
                    for kc in range(NKC):
                        ins = t_.transpose(out=ps_t1[:, kc * 128:(kc + 1) * 128],
                                           in_=q_bf[:, kc * 128:(kc + 1) * 128],
                                           identity=ident[:])
                    ins.then_inc(S["c_qtr"], 1)

                @blk.vector
                def _(v, i=i):
                    v.wait_ge(S["c_qtr"], i + 1)
                    if i >= 2:
                        v.wait_ge(S["SPQ"], 16 * (i - 1))
                    v.tensor_copy(out=qT_bf[i % 2][:],
                                  in_=ps_t1[:]).then_inc(S["c_qtev"], 1)

                @blk.tensor
                def _(t_, i=i):
                    t_.wait_ge(S["c_ktev"], i + 1)
                    t_.wait_ge(S["c_xq"], i + 1)
                    for kc in range(NKC):
                        ins = t_.transpose(out=ps_t2[:, kc * 128:(kc + 1) * 128],
                                           in_=xq_bf[:, kc * 128:(kc + 1) * 128],
                                           identity=ident[:])
                    ins.then_inc(S["c_xqtr"], 1)

                @blk.vector
                def _(v, i=i):
                    v.wait_ge(S["c_xqtr"], i + 1)
                    if i >= 2:
                        v.wait_ge(S["SPXQ"], 16 * (i - 1))
                    v.tensor_copy(out=xqT_bf[i % 2][:],
                                  in_=ps_t2[:]).then_inc(S["c_xqtev"], 1)

                for t in wtiles:
                    src, wcols, dstT, nblk = wt_src(t)
                    nch = CH[t]
                    cw = wcols // nch
                    nb = cw // 16

                    @blk.gpsimd
                    def _(g, t=t, src=src, wcols=wcols):
                        if t >= 1:
                            g.wait_ge(S["w_u"], CU[t])
                        g.dma_start(out=wstage[:, :wcols],
                                    in_=src).then_inc(S["SWT"], 16)

                    for c in range(nch):
                        @blk.vector
                        def _(v, t=t, c=c, cw=cw, nb=nb):
                            if c == 0:
                                v.wait_ge(S["SWT"], 16 * (t + 1))
                                if t >= 1:
                                    v.wait_ge(S["w_tr"], CB[t])
                            w3 = wstage[:, c * cw:(c + 1) * cw].rearrange(
                                "p (nb b) -> p nb b", b=16)
                            _quant_head(v, w3, m_s, s_s, r_s, nb, u_b,
                                        mk_vbar(v)).then_inc(S["w_u"], 1)

                        @blk.vector
                        def _(v, t=t, c=c, cw=cw, nb=nb):
                            v.wait_ge(S["w_u"], CU[t] + c + 1)
                            _quant_chain(
                                v, u_b, s_s, nb,
                                wq_bf[:, c * cw:(c + 1) * cw].rearrange(
                                    "p (nb b) -> p nb b", b=16),
                                cA, cB, usq, sgn).then_inc(S["w_q"], 1)

                    for b in range(NBAT[t]):
                        lo, hi = b * 8, min(nblk, b * 8 + 8)

                        @blk.tensor
                        def _(t_, t=t, b=b, lo=lo, hi=hi, i=i):
                            t_.wait_ge(S["w_q"], CU[t] + CH[t])
                            if b == 0 and t == wtiles[0]:
                                t_.wait_ge(S["c_qtev"], i + 1)
                            if CB[t] + b > 0:
                                t_.wait_ge(S["w_tev"], CB[t] + b)
                            for kk in range(lo, hi):
                                ins = t_.transpose(
                                    out=ps_t1[:, (kk - lo) * 128:(kk - lo + 1) * 128],
                                    in_=wq_bf[:, kk * 128:(kk + 1) * 128],
                                    identity=ident[:])
                            ins.then_inc(S["w_tr"], 1)

                        @blk.vector
                        def _(v, t=t, b=b, lo=lo, hi=hi):
                            v.wait_ge(S["w_tr"], CB[t] + b + 1)
                            if t >= 1 and b == 0:
                                v.wait_ge(S["SWW"], 16 * t)
                            v.tensor_copy(
                                out=wT_st[:, lo * 128:hi * 128],
                                in_=ps_t1[:, 0:(hi - lo) * 128]
                            ).then_inc(S["w_tev"], 1)

                    @blk.gpsimd
                    def _(g, t=t, dstT=dstT, nblk=nblk):
                        g.wait_ge(S["w_tev"], CB[t] + NBAT[t])
                        g.dma_start(out=dstT[:, :],
                                    in_=wT_st[:, :nblk * 128]).then_inc(S["SWW"], 16)

        with contextlib.ExitStack() as st2:
            E2 = st2.enter_context
            v_sb = E2(nc.sbuf_tensor("v_sb", [128, S_tok * NKC], BF16))
            wo_sb = E2(nc.sbuf_tensor("wo_sb", [128, NKC * 1024], BF16))
            S_row = E2(nc.sbuf_tensor("S_row", [128, S_tok], F32))
            P_bf = [E2(nc.sbuf_tensor(f"P_bf{j}", [128, S_tok], BF16))
                    for j in range(2)]
            PT_sb = E2(nc.sbuf_tensor("PT_sb", [128, S_tok], BF16))
            qT_t = [E2(nc.sbuf_tensor(f"qT_t{j}", [128, D], BF16)) for j in range(2)]
            o_bf = E2(nc.sbuf_tensor("o_bf", [128, D], BF16))
            oT_sb = E2(nc.sbuf_tensor("oT_sb", [128, D], BF16))
            s_bf = [E2(nc.sbuf_tensor(f"s_bf{j}", [128, D], BF16)) for j in range(2)]
            ps_s = [E2(nc.psum_tensor(f"ps_s{j}", [128, 1024], F32)) for j in range(2)]
            ps_pt = E2(nc.psum_tensor("ps_pt", [128, 1024], BF16))
            ps_o = E2(nc.psum_tensor("ps_o", [128, 1024], F32))
            ps_ot = E2(nc.psum_tensor("ps_ot", [128, 1024], BF16))
            NQ = S_tok // 1024
            NJC = S_tok // 128
            NPB = NJC // 8

            CBT = CB[NW]

            @blk.gpsimd
            def _(g):
                g.wait_ge(S["c_xqtev"], NT)
                if CBT:
                    g.wait_ge(S["w_tev"], CBT)
                    g.wait_ge(S["w_tr"], CBT)
                g.wait_ge(S["c_xqtr"], NT)
                g.wait_ge(S["SPV"], 16 * NT)
                g.wait_ge(S["SPQ"], 16 * NT)
                g.wait_ge(S["SPXQ"], 16 * NT)
                for kc in range(NKC):
                    g.dma_start(out=wo_sb[:, kc * 1024:(kc + 1) * 1024],
                                in_=wo_d[kc * 128:(kc + 1) * 128, :]
                                ).then_inc(S["SWO"], 16)

            @blk.sync
            def _(sy):
                sy.wait_ge(S["SPV"], 16 * NT)
                sy.wait_ge(S["SPQ"], 16 * NT)
                sy.wait_ge(S["SPXQ"], 16 * NT)
                sy.wait_ge(S["c_xqtev"], NT)
                if CBT:
                    sy.wait_ge(S["w_tev"], CBT)
                    sy.wait_ge(S["w_tr"], CBT)
                sy.wait_ge(S["c_xqtr"], NT)
                for j in range(NT):
                    sy.dma_start(
                        out=v_sb[:].rearrange("p (j c) -> p j c", c=1024)[:, j, :],
                        in_=v_dd[j]).then_inc(S["SVR"], 16)

            for i in range(NT + 1):
                if i < NT:
                    @blk.sync
                    def _(sy, i=i):
                        sy.wait_ge(S["SPQ"], 16 * (i + 1))
                        if i >= 2:
                            sy.wait_ge(S["p_smm"], i - 1)
                        sy.dma_start(out=qT_t[i % 2][:],
                                     in_=qT_d[i]).then_inc(S["SQR"], 16)

                    for qq in range(NQ):
                        @blk.tensor
                        def _(t_, i=i, qq=qq):
                            if serial & 2 and qq == 0 and i > 0:
                                t_.wait_ge(S["p_sev"], i)
                            if i == 0 and qq == 0:
                                t_.wait_ge(S["SVR"], 16 * NT)
                                t_.wait_ge(S["c_ktev"], NT)
                                t_.wait_ge(S["c_xqtev"], NT)
                                if CBT:
                                    t_.wait_ge(S["w_tev"], CBT)
                            if qq == 0:
                                t_.wait_ge(S["SQR"], 16 * (i + 1))
                            prev = NQ * i + qq - (2 if NQ >= 2 else 1) + 1
                            if prev > 0:
                                t_.wait_ge(S["p_Sev"], prev)
                                t_.wait_ge(S["p_max"], prev)
                            for dc in range(NKC):
                                for sl in range(2):
                                    col = dc * S_tok + qq * 1024 + sl * 512
                                    ins = t_.matmul(
                                        out=ps_s[qq % 2][:, sl * 512:(sl + 1) * 512],
                                        lhsT=qT_t[i % 2][:, dc * 128:(dc + 1) * 128],
                                        rhs=kT_sb[:, col:col + 512],
                                        start=(dc == 0), stop=(dc == NKC - 1))
                            ins.then_inc(S["p_Sq"], 1)

                        @blk.scalar
                        def _(s, i=i, qq=qq):
                            s.wait_ge(S["p_Sq"], NQ * i + qq + 1)
                            s.activation(out=S_row[:, qq * 1024:(qq + 1) * 1024],
                                         in_=ps_s[qq % 2][:],
                                         func=AF.Identity).then_inc(S["p_Sev"], 1)

                        @blk.vector
                        def _(v, i=i, qq=qq):
                            v.wait_ge(S["p_Sq"], NQ * i + qq + 1)
                            v.tensor_reduce(out=stats[:, 7 + qq:8 + qq],
                                            in_=ps_s[qq % 2][:], axis=AX, op=A.max,
                                            negate=True).then_inc(S["p_max"], 1)

                    @blk.vector
                    def _(v, i=i):
                        v.wait_ge(S["p_max"], NQ * (i + 1))
                        vbar = mk_vbar(v)
                        if NQ >= 2:
                            vbar(v.tensor_tensor(out=stats[:, 7:8],
                                                 in0=stats[:, 7:8],
                                                 in1=stats[:, 8:9], op=A.min))
                        if NQ >= 4:
                            vbar(v.tensor_tensor(out=stats[:, 9:10],
                                                 in0=stats[:, 9:10],
                                                 in1=stats[:, 10:11], op=A.min))
                            vbar(v.tensor_tensor(out=stats[:, 7:8],
                                                 in0=stats[:, 7:8],
                                                 in1=stats[:, 9:10], op=A.min))
                        bcol = 11 + 2 * (i % 2)
                        v.tensor_scalar_mul(out=stats[:, bcol:bcol + 1],
                                            in0=stats[:, 7:8],
                                            scalar1=8.0).then_inc(S["p_bias"], 1)

                    @blk.scalar
                    def _(s, i=i):
                        s.wait_ge(S["p_bias"], i + 1)
                        if i >= 2:
                            s.wait_ge(S["p_pttr"], NPB * (i - 1))
                        bcol = 11 + 2 * (i % 2)
                        rcol = 5 + 7 * (i % 2)
                        s.activation(out=P_bf[i % 2][:], in_=S_row[:], func=AF.Exp,
                                     bias=stats[:, bcol:bcol + 1], scale=8.0,
                                     accum_out=stats[:, rcol:rcol + 1]
                                     ).then_inc(S["p_exp"], 1)

                if i >= 1:
                    j = i - 1
                    for b in range(NPB):
                        @blk.tensor
                        def _(t_, j=j, b=b):
                            if b == 0:
                                t_.wait_ge(S["p_exp"], j + 1)
                            prev = NPB * j + b
                            if prev > 0:
                                t_.wait_ge(S["p_ptev"], prev)
                            for k in range(8):
                                jc = b * 8 + k
                                ins = t_.transpose(
                                    out=ps_pt[:, k * 128:(k + 1) * 128],
                                    in_=P_bf[j % 2][:, jc * 128:(jc + 1) * 128],
                                    identity=ident[:])
                            ins.then_inc(S["p_pttr"], 1)

                        @blk.vector
                        def _(v, j=j, b=b):
                            v.wait_ge(S["p_pttr"], NPB * j + b + 1)
                            v.tensor_copy(out=PT_sb[:, b * 1024:(b + 1) * 1024],
                                          in_=ps_pt[:]).then_inc(S["p_ptev"], 1)

                    @blk.tensor
                    def _(t_, j=j):
                        if j > 0:
                            t_.wait_ge(S["p_sev"], j)
                        for b in range(NPB):
                            t_.wait_ge(S["p_ptev"], NPB * j + b + 1)
                            for k in range(8):
                                jc = b * 8 + k
                                for dh in range(2):
                                    col = jc * 1024 + dh * 512
                                    ins = t_.matmul(
                                        out=ps_o[:, dh * 512:(dh + 1) * 512],
                                        lhsT=PT_sb[:, jc * 128:(jc + 1) * 128],
                                        rhs=v_sb[:, col:col + 512],
                                        start=(jc == 0), stop=(jc == NJC - 1))
                        ins.then_inc(S["p_pv"], 1)

                    @blk.vector
                    def _(v, j=j):
                        v.wait_ge(S["p_exp"], j + 1)
                        vbar = mk_vbar(v)
                        rcol = 5 + 7 * (j % 2)
                        vbar(v.reciprocal(out=stats[:, 6:7],
                                          in_=stats[:, rcol:rcol + 1]))
                        v.wait_ge(S["p_pv"], j + 1)
                        v.tensor_scalar_mul(
                            out=o_bf[:], in0=ps_o[:],
                            scalar1=stats[:, 6:7]).then_inc(S["p_oev"], 1)

                    @blk.tensor
                    def _(t_, j=j):
                        t_.wait_ge(S["p_oev"], j + 1)
                        if j > 0:
                            t_.wait_ge(S["p_otev"], j)
                        for kc in range(NKC):
                            ins = t_.transpose(out=ps_ot[:, kc * 128:(kc + 1) * 128],
                                               in_=o_bf[:, kc * 128:(kc + 1) * 128],
                                               identity=ident[:])
                        ins.then_inc(S["p_ottr"], 1)

                    @blk.vector
                    def _(v, j=j):
                        v.wait_ge(S["p_ottr"], j + 1)
                        v.tensor_copy(out=oT_sb[:],
                                      in_=ps_ot[:]).then_inc(S["p_otev"], 1)

                    @blk.tensor
                    def _(t_, j=j):
                        if j == 0:
                            t_.wait_ge(S["SWO"], 16 * NKC)
                        t_.wait_ge(S["p_otev"], j + 1)
                        for dc in range(NKC):
                            for sl in range(2):
                                col = dc * 1024 + sl * 512
                                ins = t_.matmul(
                                    out=ps_o[:, sl * 512:(sl + 1) * 512],
                                    lhsT=oT_sb[:, dc * 128:(dc + 1) * 128],
                                    rhs=wo_sb[:, col:col + 512],
                                    start=(dc == 0), stop=(dc == NKC - 1))
                        ins.then_inc(S["p_smm"], 1)

                    @blk.vector
                    def _(v, j=j):
                        v.wait_ge(S["p_smm"], j + 1)
                        if j >= 2:
                            v.wait_ge(S["SSW"], 16 * (j - 1))
                        v.tensor_copy(out=s_bf[j % 2][:],
                                      in_=ps_o[:]).then_inc(S["p_sev"], 1)

                    @blk.scalar
                    def _(s, j=j):
                        s.wait_ge(S["p_sev"], j + 1)
                        s.dma_start(out=s_dd[j],
                                    in_=s_bf[j % 2][:]).then_inc(S["SSW"], 16)

        st_kt.close()

        with contextlib.ExitStack() as st3:
            E3 = st3.enter_context
            wgT_sb = E3(nc.sbuf_tensor("wgT_sb", [128, NIC * 1024], BF16))
            wuT_sb = E3(nc.sbuf_tensor("wuT_sb", [128, NIC * 1024], BF16))
            wdT_sb = E3(nc.sbuf_tensor("wdT_sb", [128, NKC * FI], BF16))
            xqT_t = [E3(nc.sbuf_tensor(f"xqT_t{j}", [128, D], BF16)) for j in range(2)]
            s_t = [E3(nc.sbuf_tensor(f"s_t{j}", [128, D], BF16)) for j in range(2)]
            x2 = [E3(nc.sbuf_tensor(f"x2_{j}", [128, D], F32)) for j in range(2)]
            h_f = E3(nc.sbuf_tensor("h_f", [128, FI], F32))
            hq_bf = [E3(nc.sbuf_tensor(f"hq_bf{j}", [128, FI], BF16))
                     for j in range(2)]
            hqT_sb = E3(nc.sbuf_tensor("hqT_sb", [128, FI], BF16))
            hg_t = E3(nc.sbuf_tensor("hg_t", [128, 512], F32))
            g_f = E3(nc.sbuf_tensor("g_f", [128, D], F32))
            out_sb = [E3(nc.sbuf_tensor(f"out_sb{j}", [128, D], F32))
                      for j in range(2)]
            QF3 = 704
            NHC = FI // QF3
            u3_b = E3(nc.sbuf_tensor("u3_b", [128, QF3], F32))
            usq3 = E3(nc.sbuf_tensor("usq3", [128, QF3], F32))
            sgn3 = E3(nc.sbuf_tensor("sgn3", [128, QF3], F32))
            cA3 = E3(nc.sbuf_tensor("cA3", [128, QF3], F32))
            cB3 = E3(nc.sbuf_tensor("cB3", [128, QF3], F32))
            msr3 = E3(nc.sbuf_tensor("msr3", [128, 3 * 48], F32))
            m3_s, s3_s, r3_s = msr3[:, 0:48], msr3[:, 48:96], msr3[:, 96:144]
            ps_g = [E3(nc.psum_tensor(f"ps_g{j}", [128, 512], F32)) for j in range(4)]
            ps_ht = E3(nc.psum_tensor("ps_ht", [128, 1024], BF16))
            ps_d = E3(nc.psum_tensor("ps_d", [128, 1024], F32))

            nwin = []
            off = 0
            while off < FI:
                w = min(512, FI - off)
                nwin.append((off, w))
                off += w
            NCW = len(nwin)
            NB3 = (NIC + 7) // 8
            wgT3 = wgT_sb[:].rearrange("p (nt kc j) -> p nt kc j", kc=NKC, j=128)
            wuT3 = wuT_sb[:].rearrange("p (nt kc j) -> p nt kc j", kc=NKC, j=128)
            wdT3 = wdT_sb[:].rearrange("p (dt ic j) -> p dt ic j", ic=NIC, j=128)
            hq_need = [min(NHC, -(-((b * 8 + 8) * 128) // QF3)) for b in range(NB3)]

            @blk.gpsimd
            def _(g):
                g.wait_ge(S["SWW"], 16 * NW)
                g.wait_ge(S["p_smm"], NT)
                g.wait_ge(S["p_sev"], NT)
                g.wait_ge(S["SSW"], 16 * NT)
                for t in range(NIC):
                    g.dma_start(out=wgT_sb[:, t * 1024:(t + 1) * 1024],
                                in_=wgT_d[t]).then_inc(S["SWR"], 16)
                for t in range(NIC):
                    g.dma_start(out=wuT_sb[:, t * 1024:(t + 1) * 1024],
                                in_=wuT_d[t]).then_inc(S["SWR"], 16)
                for t in range(NKC):
                    g.dma_start(out=wdT_sb[:, t * FI:(t + 1) * FI],
                                in_=wdT_d[t]).then_inc(S["SWR"], 16)

            for i in range(NT + 1):
                if i >= 1:
                    j = i - 1
                    for b in range(NB3):
                        lo, hi = b * 8, min(NIC, b * 8 + 8)

                        @blk.tensor
                        def _(t_, j=j, b=b, lo=lo, hi=hi):
                            t_.wait_ge(S["f_hq"], NHC * j + hq_need[b])
                            prev = NB3 * j + b
                            if prev > 0:
                                t_.wait_ge(S["f_htev"], prev)
                            for kk in range(lo, hi):
                                ins = t_.transpose(
                                    out=ps_ht[:, (kk - lo) * 128:(kk - lo + 1) * 128],
                                    in_=hq_bf[j % 2][:, kk * 128:(kk + 1) * 128],
                                    identity=ident[:])
                            ins.then_inc(S["f_httr"], 1)

                        @blk.vector
                        def _(v, j=j, b=b, lo=lo, hi=hi):
                            v.wait_ge(S["f_httr"], NB3 * j + b + 1)
                            v.tensor_copy(out=hqT_sb[:, lo * 128:hi * 128],
                                          in_=ps_ht[:, 0:(hi - lo) * 128]
                                          ).then_inc(S["f_htev"], 1)

                    @blk.tensor
                    def _(t_, j=j):
                        t_.wait_ge(S["f_htev"], NB3 * (j + 1))
                        if j > 0:
                            t_.wait_ge(S["f_gsil"], j)
                        for ic in range(NIC):
                            for sl in range(2):
                                ins = t_.matmul(
                                    out=ps_d[:, sl * 512:(sl + 1) * 512],
                                    lhsT=hqT_sb[:, ic * 128:(ic + 1) * 128],
                                    rhs=wdT3[:, sl * 4:sl * 4 + 4, ic, :],
                                    start=(ic == 0), stop=(ic == NIC - 1))
                        ins.then_inc(S["f_down"], 1)

                    @blk.scalar
                    def _(s, j=j):
                        s.wait_ge(S["f_down"], j + 1)
                        if j > 0:
                            s.wait_ge(S["f_comb"], j)
                        s.activation(out=g_f[:], in_=ps_d[:],
                                     func=AF.Silu).then_inc(S["f_gsil"], 1)

                if i >= 1:
                    j = i - 1
                    @blk.vector
                    def _(v, j=j):
                        v.wait_ge(S["f_gsil"], j + 1)
                        v.wait_ge(S["SSR"], 16 * (j + 1))
                        v.wait_ge(S["SX2"], 16 * (j + 1))
                        if j >= 2:
                            v.wait_ge(S["SY"], 16 * (j - 1))
                        v.tensor_tensor(out=g_f[:], in0=g_f[:], in1=s_t[j % 2][:],
                                        op=A.mult)
                        v.tensor_tensor(out=out_sb[j % 2][:], in0=g_f[:],
                                        in1=x2[j % 2][:], op=A.add
                                        ).then_inc(S["f_comb"], 1)

                    @blk.scalar
                    def _(s, j=j):
                        s.wait_ge(S["f_comb"], j + 1)
                        s.dma_start(out=y_d[j * 128:(j + 1) * 128, :],
                                    in_=out_sb[j % 2][:]).then_inc(S["SY"], 16)

                if i < NT:
                    @blk.sync
                    def _(sy, i=i):
                        if i == 0:
                            sy.wait_ge(S["p_smm"], NT)
                            sy.wait_ge(S["p_sev"], NT)
                        sy.wait_ge(S["SPXQ"], 16 * (i + 1))
                        if i >= 2:
                            sy.wait_ge(S["f_gu"], NCW * (i - 1))
                        sy.dma_start(out=xqT_t[i % 2][:],
                                     in_=xqT_d[i]).then_inc(S["SXQ"], 16)
                        sy.wait_ge(S["SSW"], 16 * (i + 1))
                        if i >= 2:
                            sy.wait_ge(S["f_comb"], i - 1)
                        sy.dma_start(out=s_t[i % 2][:],
                                     in_=s_dd[i]).then_inc(S["SSR"], 16)
                        sy.dma_start(out=x2[i % 2][:],
                                     in_=x_d[i * 128:(i + 1) * 128, :]
                                     ).then_inc(S["SX2"], 16)

                    for ci, (noff, nw) in enumerate(nwin):
                        pair = ci % 2

                        @blk.tensor
                        def _(t_, i=i, ci=ci, noff=noff, nw=nw, pair=pair):
                            if serial & 4 and ci == 0 and i > 0:
                                t_.wait_ge(S["f_comb"], i)
                            if i == 0 and ci == 0:
                                t_.wait_ge(S["SWR"], 16 * (2 * NIC + NKC))
                                t_.wait_ge(S["SSW"], 16 * NT)
                                t_.wait_ge(S["p_sev"], NT)
                            if ci == 0:
                                t_.wait_ge(S["SXQ"], 16 * (i + 1))
                            prev = NCW * i + ci - 1
                            if prev > 0:
                                t_.wait_ge(S["f_hmul"], prev)
                                t_.wait_ge(S["f_sil"], prev)
                            nt0, ntn = noff // 128, nw // 128
                            for kc in range(NKC):
                                t_.matmul(out=ps_g[2 * pair][:, :nw],
                                          lhsT=xqT_t[i % 2][:, kc * 128:(kc + 1) * 128],
                                          rhs=wgT3[:, nt0:nt0 + ntn, kc, :],
                                          start=(kc == 0), stop=(kc == NKC - 1))
                                ins = t_.matmul(
                                    out=ps_g[2 * pair + 1][:, :nw],
                                    lhsT=xqT_t[i % 2][:, kc * 128:(kc + 1) * 128],
                                    rhs=wuT3[:, nt0:nt0 + ntn, kc, :],
                                    start=(kc == 0), stop=(kc == NKC - 1))
                            ins.then_inc(S["f_gu"], 1)

                        @blk.scalar
                        def _(s, i=i, ci=ci, nw=nw, pair=pair):
                            s.wait_ge(S["f_gu"], NCW * i + ci + 1)
                            prev = NCW * i + ci
                            if prev > 0:
                                s.wait_ge(S["f_hmul"], prev)
                            s.activation(out=hg_t[:, :nw], in_=ps_g[2 * pair][:, :nw],
                                         func=AF.Silu).then_inc(S["f_sil"], 1)

                        @blk.vector
                        def _(v, i=i, ci=ci, noff=noff, nw=nw, pair=pair):
                            v.wait_ge(S["f_sil"], NCW * i + ci + 1)
                            v.tensor_tensor(out=h_f[:, noff:noff + nw],
                                            in0=hg_t[:, :nw],
                                            in1=ps_g[2 * pair + 1][:, :nw],
                                            op=A.mult).then_inc(S["f_hmul"], 1)

                        for c in range(NHC):
                            if -(-((c + 1) * QF3) // 512) == ci + 1:
                                @blk.vector
                                def _(v, i=i, c=c, ci=ci):
                                    if c == 0 and i >= 2:
                                        v.wait_ge(S["f_httr"], NB3 * (i - 1))
                                    v.wait_ge(S["f_hmul"], NCW * i + ci + 1)
                                    nb = QF3 // 16
                                    h3 = h_f[:, c * QF3:(c + 1) * QF3].rearrange(
                                        "p (nb b) -> p nb b", b=16)
                                    _quant_head(v, h3, m3_s, s3_s, r3_s, nb,
                                                u3_b, mk_vbar(v)
                                                ).then_inc(S["f_hu"], 1)
                                    v.wait_ge(S["f_hu"], NHC * i + c + 1)
                                    _quant_chain(
                                        v, u3_b, s3_s, nb,
                                        hq_bf[i % 2][:, c * QF3:(c + 1) * QF3
                                                     ].rearrange(
                                            "p (nb b) -> p nb b", b=16),
                                        cA3, cB3, usq3, sgn3
                                    ).then_inc(S["f_hq"], 1)


            @blk.scalar
            def _(s):
                s.wait_ge(S["SY"], 16 * NT)

    return nc


def kernel(L0_attn_x, L0_attn_Wq, L0_attn_Wk, L0_attn_Wv, L0_attn_Wo,
           L0_ffn_w_gate, L0_ffn_w_up, L0_ffn_w_down):
    x = np.ascontiguousarray(np.asarray(L0_attn_x, dtype=np.float32))
    B, S_tok, _ = x.shape
    ident = np.eye(128, dtype=ml_dtypes.bfloat16)
    weights = dict(
        wq=np.ascontiguousarray(np.asarray(L0_attn_Wq, dtype=np.float32)),
        wk=np.ascontiguousarray(np.asarray(L0_attn_Wk, dtype=np.float32)),
        wv=np.ascontiguousarray(np.asarray(L0_attn_Wv, dtype=np.float32)),
        wo=np.ascontiguousarray(np.asarray(L0_attn_Wo, dtype=np.float32)),
        wg=np.ascontiguousarray(np.asarray(L0_ffn_w_gate, dtype=np.float32)),
        wu=np.ascontiguousarray(np.asarray(L0_ffn_w_up, dtype=np.float32)),
        wd=np.ascontiguousarray(np.asarray(L0_ffn_w_down, dtype=np.float32)),
        ident=ident)
    nc = build(S_tok)
    in_maps = [dict(x=np.ascontiguousarray(x[c]), **weights) for c in range(B)]
    res = run_bass_kernel_spmd(nc, in_maps, core_ids=list(range(B)))
    return np.stack([np.asarray(r["y"], dtype=np.float32) for r in res.results],
                    axis=0)


# version 14
# speedup vs baseline: 1.1891x; 1.1748x over previous
import contextlib
import numpy as np
import ml_dtypes
import concourse.bass as bass
import concourse.mybir as mybir
from concourse.bass_utils import run_bass_kernel_spmd
from concourse.alu_op_type import AluOpType as A

F32 = mybir.dt.float32
BF16 = mybir.dt.bfloat16
AF = mybir.ActivationFunctionType
AX = mybir.AxisListType.X

D = 1024
FI = 2816
NKC = D // 128
NIC = FI // 128


def _quant_chain(v, u_b, s_small, nb, outcols, cA, cB, ab, sg2):
    F = nb * 16
    v.tensor_tensor(out=ab[:, :F], in0=u_b[:, :F], in1=u_b[:, :F], op=A.mult)
    v.tensor_scalar(out=sg2[:, :F], in0=u_b[:, :F], scalar1=0.0, scalar2=2.0,
                    op0=A.is_gt, op1=A.mult)
    v.tensor_single_scalar(out=cA[:, :F], in_=ab[:, :F], scalar=0.0625,
                           op=A.is_gt)
    for t2 in (0.5625, 1.5625, 3.0625):
        v.scalar_tensor_tensor(out=cA[:, :F], in0=ab[:, :F], scalar=t2,
                               in1=cA[:, :F], op0=A.is_gt, op1=A.add)
    v.tensor_scalar(out=cB[:, :F], in0=ab[:, :F], scalar1=25.0, scalar2=2.0,
                    op0=A.is_gt, op1=A.mult)
    for t2 in (12.25, 6.25):
        v.scalar_tensor_tensor(out=cB[:, :F], in0=ab[:, :F], scalar=t2,
                               in1=cB[:, :F], op0=A.is_gt, op1=A.add)
    v.scalar_tensor_tensor(out=cA[:, :F], in0=cA[:, :F], scalar=0.5,
                           in1=cB[:, :F], op0=A.mult, op1=A.add)
    v.scalar_tensor_tensor(out=cA[:, :F], in0=sg2[:, :F], scalar=-1.0,
                           in1=cA[:, :F], op0=A.add, op1=A.mult)
    sb3 = s_small[:, :nb].unsqueeze(2).broadcast_to([128, nb, 16])
    return v.tensor_tensor(
        out=outcols, in0=cA[:, :F].rearrange("p (nb b) -> p nb b", b=16),
        in1=sb3, op=A.mult)


def _quant_head(v, src3, m_s, s_s, r_s, nb, u_b, vbar):
    vbar(v.tensor_reduce(out=m_s[:, :nb], in_=src3, axis=AX, op=A.max,
                         apply_absolute_value=True))
    vbar(v.tensor_scalar(out=s_s[:, :nb], in0=m_s[:, :nb], scalar1=1.0 / 6.0,
                         scalar2=1e-12, op0=A.mult, op1=A.max))
    vbar(v.reciprocal(out=r_s[:, :nb], in_=s_s[:, :nb]))
    return v.tensor_tensor(
        out=u_b[:, :nb * 16].rearrange("p (nb b) -> p nb b", b=16), in0=src3,
        in1=r_s[:, :nb].unsqueeze(2).broadcast_to([128, nb, 16]), op=A.mult)


def build(S_tok=4096, dbg=False, serial=0):
    NT = S_tok // 128
    nc = bass.Bass("TRN2", target_bir_lowering=False, detect_race_conditions=False)
    KIND = "ExternalOutput" if dbg else "Internal"


    x_d = nc.dram_tensor("x", [S_tok, D], F32, kind="ExternalInput").ap()
    wq_d = nc.dram_tensor("wq", [D, D], F32, kind="ExternalInput").ap()
    wk_d = nc.dram_tensor("wk", [D, D], F32, kind="ExternalInput").ap()
    wv_d = nc.dram_tensor("wv", [D, D], F32, kind="ExternalInput").ap()
    wo_d = nc.dram_tensor("wo", [D, D], F32, kind="ExternalInput").ap()
    wsl_d = nc.dram_tensor("wsl", [128, 66 * 128], F32, kind="ExternalInput").ap()
    id_d = nc.dram_tensor("ident", [128, 128], BF16, kind="ExternalInput").ap()
    y_d = nc.dram_tensor("y", [S_tok, D], F32, kind="ExternalOutput").ap()
    qT_d = nc.dram_tensor("qT_s", [NT, 128, D], BF16, kind=KIND).ap()
    xqT_d = nc.dram_tensor("xqT_s", [NT, 128, D], BF16, kind=KIND).ap()
    v_dd = nc.dram_tensor("v_s", [NT, 128, D], BF16, kind=KIND).ap()
    s_dd = nc.dram_tensor("s_s", [NT, 128, D], BF16, kind=KIND).ap()
    wTl_d = nc.dram_tensor("wTl_s", [128, 66 * 128], BF16, kind="Internal").ap()
    wTg_d = nc.dram_tensor("wTg_s", [8, 128, 66 * 128], BF16, kind=KIND,
                           addr_space="Shared").ap()

    NWU = 6
    WUCOLS = 1408
    wu_of = [[] for _ in range(NT)]
    for u in range(NWU):
        wu_of[min(NT - 1, 2 + u * 5)].append(u)
    NBAT_U = 2
    CBT = NWU * NBAT_U
    CB_tile = []
    acc = 0
    for i in range(NT):
        acc += NBAT_U * len(wu_of[i])
        CB_tile.append(acc)

    st_all = contextlib.ExitStack()
    with st_all:
        E = st_all.enter_context
        SEM = {}
        for name in ("SX", "SWQ", "SWT", "SWW", "SPV", "SPQ", "SPXQ",
                     "c_xread", "c_ssq", "c_var", "c_std", "c_xnbf", "c_usgn",
                     "c_u", "c_xnT", "c_xq", "c_kqv", "c_ktev", "c_qtev",
                     "c_xqtev", "c_xntr", "c_qkv", "c_ktr", "c_qtr", "c_xqtr",
                     "w_u", "w_usgn", "w_q", "w_tr", "w_tev", "CC",
                     "SQR", "SVR", "SWO", "SSW",
                     "p_Sq", "p_Sev", "p_max", "p_bias", "p_exp", "p_pttr",
                     "p_ptev", "p_pv", "p_oev", "p_ottr", "p_otev", "p_smm",
                     "p_sev",
                     "SWR", "SXQ", "SSR", "SX2", "SY",
                     "f_gu", "f_sil", "f_hmul", "f_hu", "f_husgn", "f_hq",
                     "f_httr", "f_htev", "f_down", "f_gsil", "f_comb"):
            SEM[name] = E(nc.semaphore(name))
        S = SEM
        VSY = E(nc.semaphore("VSY"))
        vsy_n = [0]

        def mk_vbar(v):
            def vbar(ins):
                vsy_n[0] += 1
                ins.then_inc(VSY, 1)
                v.wait_ge(VSY, vsy_n[0])
            return vbar

        ident = E(nc.sbuf_tensor("ident_sb", [128, 128], BF16))
        stats = E(nc.sbuf_tensor("stats", [128, 16], F32))
        st_kt = contextlib.ExitStack()
        kT_sb = st_kt.enter_context(
            nc.sbuf_tensor("kT_sb", [128, NKC * S_tok], BF16, side="right"))
        blk = E(nc.Block())

        @blk.sync
        def _(sy):
            sy.dma_start(out=ident[:], in_=id_d).then_inc(S["SX"], 16)

        with contextlib.ExitStack() as st1:
            E1 = st1.enter_context
            wqkv = E1(nc.sbuf_tensor("wqkv", [128, NKC * 3072], BF16))
            xbuf = [E1(nc.sbuf_tensor(f"xbuf{j}", [128, D], F32)) for j in range(2)]
            xc = E1(nc.sbuf_tensor("xc", [128, D], F32))
            xn = E1(nc.sbuf_tensor("xn", [128, D], F32))
            xn_bf = E1(nc.sbuf_tensor("xn_bf", [128, D], BF16))
            xnT = E1(nc.sbuf_tensor("xnT", [128, D], BF16))
            QF = 1408
            u_b = E1(nc.sbuf_tensor("u_b", [128, QF], F32))
            usq = E1(nc.sbuf_tensor("usq", [128, QF], F32))
            sgn = E1(nc.sbuf_tensor("sgn", [128, QF], F32))
            cA = E1(nc.sbuf_tensor("cA", [128, QF], F32))
            cB = E1(nc.sbuf_tensor("cB", [128, QF], F32))
            msr = E1(nc.sbuf_tensor("msr", [128, 3 * 96], F32))
            m_s, s_s, r_s = msr[:, 0:96], msr[:, 96:192], msr[:, 192:288]
            xq_bf = E1(nc.sbuf_tensor("xq_bf", [128, D], BF16))
            q_bf = E1(nc.sbuf_tensor("q_bf", [128, D], BF16))
            k_bf = E1(nc.sbuf_tensor("k_bf", [128, D], BF16))
            v_bf = [E1(nc.sbuf_tensor(f"v_bf{j}", [128, D], BF16)) for j in range(2)]
            qT_bf = [E1(nc.sbuf_tensor(f"qT_bf{j}", [128, D], BF16)) for j in range(2)]
            xqT_bf = [E1(nc.sbuf_tensor(f"xqT_bf{j}", [128, D], BF16)) for j in range(2)]
            wstage = E1(nc.sbuf_tensor("wstage", [128, 1408], F32))
            wq_bf = E1(nc.sbuf_tensor("wq_bf", [128, 1408], BF16))
            wT_st = E1(nc.sbuf_tensor("wT_st", [128, 1408], BF16))
            ps_qkv = [E1(nc.psum_tensor(f"ps_qkv{j}", [128, 512], F32))
                      for j in range(6)]
            ps_t1 = E1(nc.psum_tensor("ps_t1", [128, 1024], BF16))
            ps_t2 = E1(nc.psum_tensor("ps_t2", [128, 1024], BF16))

            @blk.gpsimd
            def _(g):
                for kc in range(NKC):
                    for wi, wsrc in enumerate((wq_d, wk_d, wv_d)):
                        c0 = kc * 3072 + wi * 1024
                        g.dma_start(out=wqkv[:, c0:c0 + 1024],
                                    in_=wsrc[kc * 128:(kc + 1) * 128, :]
                                    ).then_inc(S["SWQ"], 16)

            for i in range(NT):
                wtiles = wu_of[i]

                @blk.sync
                def _(sy, i=i):
                    if i >= 2:
                        sy.wait_ge(S["c_xread"], i - 1)
                    sy.dma_start(out=xbuf[i % 2][:],
                                 in_=x_d[i * 128:(i + 1) * 128, :]
                                 ).then_inc(S["SX"], 16)

                @blk.vector
                def _(v, i=i):
                    if serial & 1 and i > 0:
                        v.wait_ge(S["c_xqtev"], i)
                        if CB_tile[i - 1] > 0:
                            v.wait_ge(S["w_tev"], CB_tile[i - 1])
                    v.wait_ge(S["SX"], 16 * (i + 2))
                    xb = xbuf[i % 2]
                    vbar = mk_vbar(v)
                    vbar(v.tensor_reduce(out=stats[:, 0:1], in_=xb[:], axis=AX,
                                         op=A.add, negate=True))
                    vbar(v.tensor_scalar_mul(out=stats[:, 1:2],
                                             in0=stats[:, 0:1], scalar1=1.0 / D))
                    v.tensor_scalar_add(out=xc[:], in0=xb[:],
                                        scalar1=stats[:, 1:2]
                                        ).then_inc(S["c_xread"], 1)

                @blk.vector
                def _(v, i=i):
                    vbar = mk_vbar(v)
                    vbar(v.scalar_tensor_tensor(out=u_b[:, :D], in0=xc[:],
                                                scalar=1.0, in1=xc[:],
                                                op0=A.mult, op1=A.mult,
                                                accum_out=stats[:, 12:13]))
                    v.tensor_scalar(out=stats[:, 2:3], in0=stats[:, 12:13],
                                    scalar1=1.0 / D, scalar2=1e-5,
                                    op0=A.mult, op1=A.add).then_inc(S["c_var"], 1)

                @blk.scalar
                def _(s, i=i):
                    s.wait_ge(S["c_var"], i + 1)
                    s.activation(out=stats[:, 3:4], in_=stats[:, 2:3],
                                 func=AF.Sqrt).then_inc(S["c_std"], 1)
                    s.wait_ge(S["c_u"], i + 1)
                    s.activation(out=xn_bf[:], in_=xc[:], func=AF.Identity,
                                 scale=stats[:, 4:5]).then_inc(S["c_xnbf"], 1)
                    s.wait_ge(S["c_kqv"], i + 1)
                    s.dma_start(out=v_dd[i], in_=v_bf[i % 2][:]).then_inc(S["SPV"], 16)
                    s.wait_ge(S["c_qtev"], i + 1)
                    s.dma_start(out=qT_d[i], in_=qT_bf[i % 2][:]).then_inc(S["SPQ"], 16)
                    s.wait_ge(S["c_xqtev"], i + 1)
                    s.dma_start(out=xqT_d[i],
                                in_=xqT_bf[i % 2][:]).then_inc(S["SPXQ"], 16)

                @blk.vector
                def _(v, i=i):
                    v.wait_ge(S["c_std"], i + 1)
                    vbar = mk_vbar(v)
                    vbar(v.reciprocal(out=stats[:, 4:5], in_=stats[:, 3:4]))
                    v.tensor_scalar_mul(out=xn[:], in0=xc[:],
                                        scalar1=stats[:, 4:5])
                    x3 = xn[:].rearrange("p (nb b) -> p nb b", b=16)
                    _quant_head(v, x3, m_s, s_s, r_s, 64, u_b,
                                vbar).then_inc(S["c_u"], 1)

                @blk.tensor
                def _(t_, i=i):
                    t_.wait_ge(S["c_xnbf"], i + 1)
                    if i > 0:
                        t_.wait_ge(S["c_qtev"], i)
                        if CB_tile[i - 1] > 0:
                            t_.wait_ge(S["w_tev"], CB_tile[i - 1])
                    for kc in range(NKC):
                        ins = t_.transpose(out=ps_t1[:, kc * 128:(kc + 1) * 128],
                                           in_=xn_bf[:, kc * 128:(kc + 1) * 128],
                                           identity=ident[:])
                    ins.then_inc(S["c_xntr"], 1)

                @blk.vector
                def _(v, i=i):
                    v.wait_ge(S["c_xntr"], i + 1)
                    v.tensor_copy(out=xnT[:], in_=ps_t1[:]).then_inc(S["c_xnT"], 1)
                    v.wait_ge(S["c_u"], i + 1)
                    _quant_chain(v, u_b, s_s, 64,
                                 xq_bf[:].rearrange("p (nb b) -> p nb b", b=16),
                                 cA, cB, usq, sgn).then_inc(S["c_xq"], 1)

                @blk.tensor
                def _(t_, i=i):
                    if i == 0:
                        t_.wait_ge(S["SWQ"], 16 * NKC * 3)
                    t_.wait_ge(S["c_xnT"], i + 1)
                    if i > 0:
                        t_.wait_ge(S["c_kqv"], i)
                    for kc in range(NKC):
                        for ns in range(6):
                            c0 = kc * 3072 + ns * 512
                            ins = t_.matmul(out=ps_qkv[ns][:],
                                            lhsT=xnT[:, kc * 128:(kc + 1) * 128],
                                            rhs=wqkv[:, c0:c0 + 512],
                                            start=(kc == 0), stop=(kc == NKC - 1))
                    ins.then_inc(S["c_qkv"], 1)

                @blk.vector
                def _(v, i=i):
                    v.wait_ge(S["c_qkv"], i + 1)
                    if i >= 2:
                        v.wait_ge(S["SPV"], 16 * (i - 1))
                    v.tensor_copy(out=q_bf[:, 0:512], in_=ps_qkv[0][:])
                    v.tensor_copy(out=q_bf[:, 512:1024], in_=ps_qkv[1][:])
                    v.tensor_copy(out=k_bf[:, 0:512], in_=ps_qkv[2][:])
                    v.tensor_copy(out=k_bf[:, 512:1024], in_=ps_qkv[3][:])
                    v.tensor_copy(out=v_bf[i % 2][:, 0:512], in_=ps_qkv[4][:])
                    v.tensor_copy(out=v_bf[i % 2][:, 512:1024],
                                  in_=ps_qkv[5][:]).then_inc(S["c_kqv"], 1)

                @blk.tensor
                def _(t_, i=i):
                    t_.wait_ge(S["c_kqv"], i + 1)
                    if i > 0:
                        t_.wait_ge(S["c_xqtev"], i)
                    for kc in range(NKC):
                        ins = t_.transpose(out=ps_t2[:, kc * 128:(kc + 1) * 128],
                                           in_=k_bf[:, kc * 128:(kc + 1) * 128],
                                           identity=ident[:])
                    ins.then_inc(S["c_ktr"], 1)

                @blk.vector
                def _(v, i=i):
                    v.wait_ge(S["c_ktr"], i + 1)
                    kdst = kT_sb[:].rearrange("p (dc s) -> p dc s", s=S_tok)[
                        :, :, i * 128:(i + 1) * 128]
                    v.tensor_copy(out=kdst, in_=ps_t2[:].rearrange(
                        "p (dc j) -> p dc j", j=128)).then_inc(S["c_ktev"], 1)

                @blk.tensor
                def _(t_, i=i):
                    for kc in range(NKC):
                        ins = t_.transpose(out=ps_t1[:, kc * 128:(kc + 1) * 128],
                                           in_=q_bf[:, kc * 128:(kc + 1) * 128],
                                           identity=ident[:])
                    ins.then_inc(S["c_qtr"], 1)

                @blk.vector
                def _(v, i=i):
                    v.wait_ge(S["c_qtr"], i + 1)
                    if i >= 2:
                        v.wait_ge(S["SPQ"], 16 * (i - 1))
                    v.tensor_copy(out=qT_bf[i % 2][:],
                                  in_=ps_t1[:]).then_inc(S["c_qtev"], 1)

                @blk.tensor
                def _(t_, i=i):
                    t_.wait_ge(S["c_ktev"], i + 1)
                    t_.wait_ge(S["c_xq"], i + 1)
                    for kc in range(NKC):
                        ins = t_.transpose(out=ps_t2[:, kc * 128:(kc + 1) * 128],
                                           in_=xq_bf[:, kc * 128:(kc + 1) * 128],
                                           identity=ident[:])
                    ins.then_inc(S["c_xqtr"], 1)

                @blk.vector
                def _(v, i=i):
                    v.wait_ge(S["c_xqtr"], i + 1)
                    if i >= 2:
                        v.wait_ge(S["SPXQ"], 16 * (i - 1))
                    v.tensor_copy(out=xqT_bf[i % 2][:],
                                  in_=ps_t2[:]).then_inc(S["c_xqtev"], 1)

                for u in wtiles:
                    @blk.gpsimd
                    def _(g, u=u):
                        if u >= 1:
                            g.wait_ge(S["w_u"], u)
                        g.dma_start(out=wstage[:, :WUCOLS],
                                    in_=wsl_d[:, u * WUCOLS:(u + 1) * WUCOLS]
                                    ).then_inc(S["SWT"], 16)

                    @blk.vector
                    def _(v, u=u):
                        v.wait_ge(S["SWT"], 16 * (u + 1))
                        if u >= 1:
                            v.wait_ge(S["w_tr"], NBAT_U * u)
                        w3 = wstage[:, :WUCOLS].rearrange(
                            "p (nb b) -> p nb b", b=16)
                        _quant_head(v, w3, m_s, s_s, r_s, WUCOLS // 16, u_b,
                                    mk_vbar(v)).then_inc(S["w_u"], 1)
                        _quant_chain(
                            v, u_b, s_s, WUCOLS // 16,
                            wq_bf[:, :WUCOLS].rearrange("p (nb b) -> p nb b", b=16),
                            cA, cB, usq, sgn).then_inc(S["w_q"], 1)

                    for b in range(NBAT_U):
                        lo, hi = b * 8, min(11, b * 8 + 8)

                        @blk.tensor
                        def _(t_, u=u, b=b, lo=lo, hi=hi, i=i):
                            t_.wait_ge(S["w_q"], u + 1)
                            if b == 0 and u == wtiles[0]:
                                t_.wait_ge(S["c_qtev"], i + 1)
                            if NBAT_U * u + b > 0:
                                t_.wait_ge(S["w_tev"], NBAT_U * u + b)
                            for kk in range(lo, hi):
                                ins = t_.transpose(
                                    out=ps_t1[:, (kk - lo) * 128:(kk - lo + 1) * 128],
                                    in_=wq_bf[:, kk * 128:(kk + 1) * 128],
                                    identity=ident[:])
                            ins.then_inc(S["w_tr"], 1)

                        @blk.vector
                        def _(v, u=u, b=b, lo=lo, hi=hi):
                            v.wait_ge(S["w_tr"], NBAT_U * u + b + 1)
                            if u >= 1 and b == 0:
                                v.wait_ge(S["SWW"], 16 * u)
                            v.tensor_copy(
                                out=wT_st[:, lo * 128:hi * 128],
                                in_=ps_t1[:, 0:(hi - lo) * 128]
                            ).then_inc(S["w_tev"], 1)

                    @blk.gpsimd
                    def _(g, u=u):
                        g.wait_ge(S["w_tev"], NBAT_U * (u + 1))
                        g.dma_start(out=wTl_d[:, u * WUCOLS:(u + 1) * WUCOLS],
                                    in_=wT_st[:, :WUCOLS]).then_inc(S["SWW"], 16)

            @blk.gpsimd
            def _(g):
                g.wait_ge(S["SWW"], 16 * NWU)
                g.collective_compute(
                    "AllGather", A.bypass,
                    replica_groups=[list(range(8))],
                    ins=[wTl_d], outs=[wTg_d.rearrange("r p c -> (r p) c")],
                ).then_inc(S["CC"], 1)

        with contextlib.ExitStack() as st2:
            E2 = st2.enter_context
            v_sb = E2(nc.sbuf_tensor("v_sb", [128, S_tok * NKC], BF16))
            wo_sb = E2(nc.sbuf_tensor("wo_sb", [128, NKC * 1024], BF16))
            S_row = E2(nc.sbuf_tensor("S_row", [128, S_tok], F32))
            P_bf = [E2(nc.sbuf_tensor(f"P_bf{j}", [128, S_tok], BF16))
                    for j in range(2)]
            PT_sb = E2(nc.sbuf_tensor("PT_sb", [128, S_tok], BF16))
            qT_t = [E2(nc.sbuf_tensor(f"qT_t{j}", [128, D], BF16)) for j in range(2)]
            o_bf = E2(nc.sbuf_tensor("o_bf", [128, D], BF16))
            oT_sb = E2(nc.sbuf_tensor("oT_sb", [128, D], BF16))
            s_bf = [E2(nc.sbuf_tensor(f"s_bf{j}", [128, D], BF16)) for j in range(2)]
            ps_s = [E2(nc.psum_tensor(f"ps_s{j}", [128, 1024], F32)) for j in range(2)]
            ps_pt = E2(nc.psum_tensor("ps_pt", [128, 1024], BF16))
            ps_o = E2(nc.psum_tensor("ps_o", [128, 1024], F32))
            ps_ot = E2(nc.psum_tensor("ps_ot", [128, 1024], BF16))
            NQ = S_tok // 1024
            NJC = S_tok // 128
            NPB = NJC // 8


            @blk.gpsimd
            def _(g):
                g.wait_ge(S["c_xqtev"], NT)
                if CBT:
                    g.wait_ge(S["w_tev"], CBT)
                    g.wait_ge(S["w_tr"], CBT)
                g.wait_ge(S["c_xqtr"], NT)
                g.wait_ge(S["SPV"], 16 * NT)
                g.wait_ge(S["SPQ"], 16 * NT)
                g.wait_ge(S["SPXQ"], 16 * NT)
                for kc in range(NKC):
                    g.dma_start(out=wo_sb[:, kc * 1024:(kc + 1) * 1024],
                                in_=wo_d[kc * 128:(kc + 1) * 128, :]
                                ).then_inc(S["SWO"], 16)

            @blk.sync
            def _(sy):
                sy.wait_ge(S["SPV"], 16 * NT)
                sy.wait_ge(S["SPQ"], 16 * NT)
                sy.wait_ge(S["SPXQ"], 16 * NT)
                sy.wait_ge(S["c_xqtev"], NT)
                if CBT:
                    sy.wait_ge(S["w_tev"], CBT)
                    sy.wait_ge(S["w_tr"], CBT)
                sy.wait_ge(S["c_xqtr"], NT)
                for j in range(NT):
                    sy.dma_start(
                        out=v_sb[:].rearrange("p (j c) -> p j c", c=1024)[:, j, :],
                        in_=v_dd[j]).then_inc(S["SVR"], 16)

            for i in range(NT + 1):
                if i < NT:
                    @blk.sync
                    def _(sy, i=i):
                        sy.wait_ge(S["SPQ"], 16 * (i + 1))
                        if i >= 2:
                            sy.wait_ge(S["p_smm"], i - 1)
                        sy.dma_start(out=qT_t[i % 2][:],
                                     in_=qT_d[i]).then_inc(S["SQR"], 16)

                    for qq in range(NQ):
                        @blk.tensor
                        def _(t_, i=i, qq=qq):
                            if serial & 2 and qq == 0 and i > 0:
                                t_.wait_ge(S["p_sev"], i)
                            if i == 0 and qq == 0:
                                t_.wait_ge(S["SVR"], 16 * NT)
                                t_.wait_ge(S["c_ktev"], NT)
                                t_.wait_ge(S["c_xqtev"], NT)
                                if CBT:
                                    t_.wait_ge(S["w_tev"], CBT)
                            if qq == 0:
                                t_.wait_ge(S["SQR"], 16 * (i + 1))
                            prev = NQ * i + qq - (2 if NQ >= 2 else 1) + 1
                            if prev > 0:
                                t_.wait_ge(S["p_Sev"], prev)
                                t_.wait_ge(S["p_max"], prev)
                            for dc in range(NKC):
                                for sl in range(2):
                                    col = dc * S_tok + qq * 1024 + sl * 512
                                    ins = t_.matmul(
                                        out=ps_s[qq % 2][:, sl * 512:(sl + 1) * 512],
                                        lhsT=qT_t[i % 2][:, dc * 128:(dc + 1) * 128],
                                        rhs=kT_sb[:, col:col + 512],
                                        start=(dc == 0), stop=(dc == NKC - 1))
                            ins.then_inc(S["p_Sq"], 1)

                        @blk.scalar
                        def _(s, i=i, qq=qq):
                            s.wait_ge(S["p_Sq"], NQ * i + qq + 1)
                            s.activation(out=S_row[:, qq * 1024:(qq + 1) * 1024],
                                         in_=ps_s[qq % 2][:],
                                         func=AF.Identity).then_inc(S["p_Sev"], 1)

                        @blk.vector
                        def _(v, i=i, qq=qq):
                            v.wait_ge(S["p_Sq"], NQ * i + qq + 1)
                            v.tensor_reduce(out=stats[:, 7 + qq:8 + qq],
                                            in_=ps_s[qq % 2][:], axis=AX, op=A.max,
                                            negate=True).then_inc(S["p_max"], 1)

                    @blk.vector
                    def _(v, i=i):
                        v.wait_ge(S["p_max"], NQ * (i + 1))
                        vbar = mk_vbar(v)
                        if NQ >= 2:
                            vbar(v.tensor_tensor(out=stats[:, 7:8],
                                                 in0=stats[:, 7:8],
                                                 in1=stats[:, 8:9], op=A.min))
                        if NQ >= 4:
                            vbar(v.tensor_tensor(out=stats[:, 9:10],
                                                 in0=stats[:, 9:10],
                                                 in1=stats[:, 10:11], op=A.min))
                            vbar(v.tensor_tensor(out=stats[:, 7:8],
                                                 in0=stats[:, 7:8],
                                                 in1=stats[:, 9:10], op=A.min))
                        bcol = 11 + 2 * (i % 2)
                        v.tensor_scalar_mul(out=stats[:, bcol:bcol + 1],
                                            in0=stats[:, 7:8],
                                            scalar1=8.0).then_inc(S["p_bias"], 1)

                    @blk.scalar
                    def _(s, i=i):
                        s.wait_ge(S["p_bias"], i + 1)
                        if i >= 2:
                            s.wait_ge(S["p_pttr"], NPB * (i - 1))
                        bcol = 11 + 2 * (i % 2)
                        rcol = 5 + 7 * (i % 2)
                        s.activation(out=P_bf[i % 2][:], in_=S_row[:], func=AF.Exp,
                                     bias=stats[:, bcol:bcol + 1], scale=8.0,
                                     accum_out=stats[:, rcol:rcol + 1]
                                     ).then_inc(S["p_exp"], 1)

                if i >= 1:
                    j = i - 1
                    for b in range(NPB):
                        @blk.tensor
                        def _(t_, j=j, b=b):
                            if b == 0:
                                t_.wait_ge(S["p_exp"], j + 1)
                            prev = NPB * j + b
                            if prev > 0:
                                t_.wait_ge(S["p_ptev"], prev)
                            for k in range(8):
                                jc = b * 8 + k
                                ins = t_.transpose(
                                    out=ps_pt[:, k * 128:(k + 1) * 128],
                                    in_=P_bf[j % 2][:, jc * 128:(jc + 1) * 128],
                                    identity=ident[:])
                            ins.then_inc(S["p_pttr"], 1)

                        @blk.vector
                        def _(v, j=j, b=b):
                            v.wait_ge(S["p_pttr"], NPB * j + b + 1)
                            v.tensor_copy(out=PT_sb[:, b * 1024:(b + 1) * 1024],
                                          in_=ps_pt[:]).then_inc(S["p_ptev"], 1)

                    @blk.tensor
                    def _(t_, j=j):
                        if j > 0:
                            t_.wait_ge(S["p_sev"], j)
                        for b in range(NPB):
                            t_.wait_ge(S["p_ptev"], NPB * j + b + 1)
                            for k in range(8):
                                jc = b * 8 + k
                                for dh in range(2):
                                    col = jc * 1024 + dh * 512
                                    ins = t_.matmul(
                                        out=ps_o[:, dh * 512:(dh + 1) * 512],
                                        lhsT=PT_sb[:, jc * 128:(jc + 1) * 128],
                                        rhs=v_sb[:, col:col + 512],
                                        start=(jc == 0), stop=(jc == NJC - 1))
                        ins.then_inc(S["p_pv"], 1)

                    @blk.vector
                    def _(v, j=j):
                        v.wait_ge(S["p_exp"], j + 1)
                        vbar = mk_vbar(v)
                        rcol = 5 + 7 * (j % 2)
                        vbar(v.reciprocal(out=stats[:, 6:7],
                                          in_=stats[:, rcol:rcol + 1]))
                        v.wait_ge(S["p_pv"], j + 1)
                        v.tensor_scalar_mul(
                            out=o_bf[:], in0=ps_o[:],
                            scalar1=stats[:, 6:7]).then_inc(S["p_oev"], 1)

                    @blk.tensor
                    def _(t_, j=j):
                        t_.wait_ge(S["p_oev"], j + 1)
                        if j > 0:
                            t_.wait_ge(S["p_otev"], j)
                        for kc in range(NKC):
                            ins = t_.transpose(out=ps_ot[:, kc * 128:(kc + 1) * 128],
                                               in_=o_bf[:, kc * 128:(kc + 1) * 128],
                                               identity=ident[:])
                        ins.then_inc(S["p_ottr"], 1)

                    @blk.vector
                    def _(v, j=j):
                        v.wait_ge(S["p_ottr"], j + 1)
                        v.tensor_copy(out=oT_sb[:],
                                      in_=ps_ot[:]).then_inc(S["p_otev"], 1)

                    @blk.tensor
                    def _(t_, j=j):
                        if j == 0:
                            t_.wait_ge(S["SWO"], 16 * NKC)
                        t_.wait_ge(S["p_otev"], j + 1)
                        for dc in range(NKC):
                            for sl in range(2):
                                col = dc * 1024 + sl * 512
                                ins = t_.matmul(
                                    out=ps_o[:, sl * 512:(sl + 1) * 512],
                                    lhsT=oT_sb[:, dc * 128:(dc + 1) * 128],
                                    rhs=wo_sb[:, col:col + 512],
                                    start=(dc == 0), stop=(dc == NKC - 1))
                        ins.then_inc(S["p_smm"], 1)

                    @blk.vector
                    def _(v, j=j):
                        v.wait_ge(S["p_smm"], j + 1)
                        if j >= 2:
                            v.wait_ge(S["SSW"], 16 * (j - 1))
                        v.tensor_copy(out=s_bf[j % 2][:],
                                      in_=ps_o[:]).then_inc(S["p_sev"], 1)

                    @blk.scalar
                    def _(s, j=j):
                        s.wait_ge(S["p_sev"], j + 1)
                        s.dma_start(out=s_dd[j],
                                    in_=s_bf[j % 2][:]).then_inc(S["SSW"], 16)

        st_kt.close()

        with contextlib.ExitStack() as st3:
            E3 = st3.enter_context
            wT_all = E3(nc.sbuf_tensor("wT_all", [128, 528 * 128], BF16))
            xqT_t = [E3(nc.sbuf_tensor(f"xqT_t{j}", [128, D], BF16)) for j in range(2)]
            s_t = [E3(nc.sbuf_tensor(f"s_t{j}", [128, D], BF16)) for j in range(2)]
            x2 = [E3(nc.sbuf_tensor(f"x2_{j}", [128, D], F32)) for j in range(2)]
            h_f = E3(nc.sbuf_tensor("h_f", [128, FI], F32))
            hq_bf = [E3(nc.sbuf_tensor(f"hq_bf{j}", [128, FI], BF16))
                     for j in range(2)]
            hqT_sb = E3(nc.sbuf_tensor("hqT_sb", [128, FI], BF16))
            hg_t = E3(nc.sbuf_tensor("hg_t", [128, 512], F32))
            g_f = E3(nc.sbuf_tensor("g_f", [128, D], F32))
            out_sb = [E3(nc.sbuf_tensor(f"out_sb{j}", [128, D], F32))
                      for j in range(2)]
            QF3 = 704
            NHC = FI // QF3
            u3_b = E3(nc.sbuf_tensor("u3_b", [128, QF3], F32))
            usq3 = E3(nc.sbuf_tensor("usq3", [128, QF3], F32))
            sgn3 = E3(nc.sbuf_tensor("sgn3", [128, QF3], F32))
            cA3 = E3(nc.sbuf_tensor("cA3", [128, QF3], F32))
            cB3 = E3(nc.sbuf_tensor("cB3", [128, QF3], F32))
            msr3 = E3(nc.sbuf_tensor("msr3", [128, 3 * 48], F32))
            m3_s, s3_s, r3_s = msr3[:, 0:48], msr3[:, 48:96], msr3[:, 96:144]
            ps_g = [E3(nc.psum_tensor(f"ps_g{j}", [128, 512], F32)) for j in range(4)]
            ps_ht = E3(nc.psum_tensor("ps_ht", [128, 1024], BF16))
            ps_d = E3(nc.psum_tensor("ps_d", [128, 1024], F32))

            nwin = []
            off = 0
            while off < FI:
                w = min(512, FI - off)
                nwin.append((off, w))
                off += w
            NCW = len(nwin)
            NB3 = (NIC + 7) // 8
            wgT3 = wT_all[:, 0:176 * 128].rearrange(
                "p (nt kc j) -> p nt kc j", kc=NKC, j=128)
            wuT3 = wT_all[:, 176 * 128:352 * 128].rearrange(
                "p (nt kc j) -> p nt kc j", kc=NKC, j=128)
            wdT3 = wT_all[:, 352 * 128:528 * 128].rearrange(
                "p (dt ic j) -> p dt ic j", ic=NIC, j=128)
            hq_need = [min(NHC, -(-((b * 8 + 8) * 128) // QF3)) for b in range(NB3)]

            @blk.gpsimd
            def _(g):
                g.wait_ge(S["CC"], 1)
                g.wait_ge(S["p_smm"], NT)
                g.wait_ge(S["p_sev"], NT)
                g.wait_ge(S["SSW"], 16 * NT)
                for r in range(8):
                    g.dma_start(out=wT_all[:, r * 66 * 128:(r + 1) * 66 * 128],
                                in_=wTg_d[r]).then_inc(S["SWR"], 16)

            for i in range(NT + 1):
                if i >= 1:
                    j = i - 1
                    for b in range(NB3):
                        lo, hi = b * 8, min(NIC, b * 8 + 8)

                        @blk.tensor
                        def _(t_, j=j, b=b, lo=lo, hi=hi):
                            t_.wait_ge(S["f_hq"], NHC * j + hq_need[b])
                            prev = NB3 * j + b
                            if prev > 0:
                                t_.wait_ge(S["f_htev"], prev)
                            for kk in range(lo, hi):
                                ins = t_.transpose(
                                    out=ps_ht[:, (kk - lo) * 128:(kk - lo + 1) * 128],
                                    in_=hq_bf[j % 2][:, kk * 128:(kk + 1) * 128],
                                    identity=ident[:])
                            ins.then_inc(S["f_httr"], 1)

                        @blk.vector
                        def _(v, j=j, b=b, lo=lo, hi=hi):
                            v.wait_ge(S["f_httr"], NB3 * j + b + 1)
                            v.tensor_copy(out=hqT_sb[:, lo * 128:hi * 128],
                                          in_=ps_ht[:, 0:(hi - lo) * 128]
                                          ).then_inc(S["f_htev"], 1)

                    @blk.tensor
                    def _(t_, j=j):
                        t_.wait_ge(S["f_htev"], NB3 * (j + 1))
                        if j > 0:
                            t_.wait_ge(S["f_gsil"], j)
                        for ic in range(NIC):
                            for sl in range(2):
                                ins = t_.matmul(
                                    out=ps_d[:, sl * 512:(sl + 1) * 512],
                                    lhsT=hqT_sb[:, ic * 128:(ic + 1) * 128],
                                    rhs=wdT3[:, sl * 4:sl * 4 + 4, ic, :],
                                    start=(ic == 0), stop=(ic == NIC - 1))
                        ins.then_inc(S["f_down"], 1)

                    @blk.scalar
                    def _(s, j=j):
                        s.wait_ge(S["f_down"], j + 1)
                        if j > 0:
                            s.wait_ge(S["f_comb"], j)
                        s.activation(out=g_f[:], in_=ps_d[:],
                                     func=AF.Silu).then_inc(S["f_gsil"], 1)

                if i >= 1:
                    j = i - 1
                    @blk.vector
                    def _(v, j=j):
                        v.wait_ge(S["f_gsil"], j + 1)
                        v.wait_ge(S["SSR"], 16 * (j + 1))
                        v.wait_ge(S["SX2"], 16 * (j + 1))
                        if j >= 2:
                            v.wait_ge(S["SY"], 16 * (j - 1))
                        v.tensor_tensor(out=g_f[:], in0=g_f[:], in1=s_t[j % 2][:],
                                        op=A.mult)
                        v.tensor_tensor(out=out_sb[j % 2][:], in0=g_f[:],
                                        in1=x2[j % 2][:], op=A.add
                                        ).then_inc(S["f_comb"], 1)

                    @blk.scalar
                    def _(s, j=j):
                        s.wait_ge(S["f_comb"], j + 1)
                        s.dma_start(out=y_d[j * 128:(j + 1) * 128, :],
                                    in_=out_sb[j % 2][:]).then_inc(S["SY"], 16)

                if i < NT:
                    @blk.sync
                    def _(sy, i=i):
                        if i == 0:
                            sy.wait_ge(S["p_smm"], NT)
                            sy.wait_ge(S["p_sev"], NT)
                        sy.wait_ge(S["SPXQ"], 16 * (i + 1))
                        if i >= 2:
                            sy.wait_ge(S["f_gu"], NCW * (i - 1))
                        sy.dma_start(out=xqT_t[i % 2][:],
                                     in_=xqT_d[i]).then_inc(S["SXQ"], 16)
                        sy.wait_ge(S["SSW"], 16 * (i + 1))
                        if i >= 2:
                            sy.wait_ge(S["f_comb"], i - 1)
                        sy.dma_start(out=s_t[i % 2][:],
                                     in_=s_dd[i]).then_inc(S["SSR"], 16)
                        sy.dma_start(out=x2[i % 2][:],
                                     in_=x_d[i * 128:(i + 1) * 128, :]
                                     ).then_inc(S["SX2"], 16)

                    for ci, (noff, nw) in enumerate(nwin):
                        pair = ci % 2

                        @blk.tensor
                        def _(t_, i=i, ci=ci, noff=noff, nw=nw, pair=pair):
                            if serial & 4 and ci == 0 and i > 0:
                                t_.wait_ge(S["f_comb"], i)
                            if i == 0 and ci == 0:
                                t_.wait_ge(S["SWR"], 16 * 8)
                                t_.wait_ge(S["SSW"], 16 * NT)
                                t_.wait_ge(S["p_sev"], NT)
                            if ci == 0:
                                t_.wait_ge(S["SXQ"], 16 * (i + 1))
                            prev = NCW * i + ci - 1
                            if prev > 0:
                                t_.wait_ge(S["f_hmul"], prev)
                                t_.wait_ge(S["f_sil"], prev)
                            nt0, ntn = noff // 128, nw // 128
                            for kc in range(NKC):
                                t_.matmul(out=ps_g[2 * pair][:, :nw],
                                          lhsT=xqT_t[i % 2][:, kc * 128:(kc + 1) * 128],
                                          rhs=wgT3[:, nt0:nt0 + ntn, kc, :],
                                          start=(kc == 0), stop=(kc == NKC - 1))
                                ins = t_.matmul(
                                    out=ps_g[2 * pair + 1][:, :nw],
                                    lhsT=xqT_t[i % 2][:, kc * 128:(kc + 1) * 128],
                                    rhs=wuT3[:, nt0:nt0 + ntn, kc, :],
                                    start=(kc == 0), stop=(kc == NKC - 1))
                            ins.then_inc(S["f_gu"], 1)

                        @blk.scalar
                        def _(s, i=i, ci=ci, nw=nw, pair=pair):
                            s.wait_ge(S["f_gu"], NCW * i + ci + 1)
                            prev = NCW * i + ci
                            if prev > 0:
                                s.wait_ge(S["f_hmul"], prev)
                            s.activation(out=hg_t[:, :nw], in_=ps_g[2 * pair][:, :nw],
                                         func=AF.Silu).then_inc(S["f_sil"], 1)

                        @blk.vector
                        def _(v, i=i, ci=ci, noff=noff, nw=nw, pair=pair):
                            v.wait_ge(S["f_sil"], NCW * i + ci + 1)
                            v.tensor_tensor(out=h_f[:, noff:noff + nw],
                                            in0=hg_t[:, :nw],
                                            in1=ps_g[2 * pair + 1][:, :nw],
                                            op=A.mult).then_inc(S["f_hmul"], 1)

                        for c in range(NHC):
                            if -(-((c + 1) * QF3) // 512) == ci + 1:
                                @blk.vector
                                def _(v, i=i, c=c, ci=ci):
                                    if c == 0 and i >= 2:
                                        v.wait_ge(S["f_httr"], NB3 * (i - 1))
                                    v.wait_ge(S["f_hmul"], NCW * i + ci + 1)
                                    nb = QF3 // 16
                                    h3 = h_f[:, c * QF3:(c + 1) * QF3].rearrange(
                                        "p (nb b) -> p nb b", b=16)
                                    _quant_head(v, h3, m3_s, s3_s, r3_s, nb,
                                                u3_b, mk_vbar(v)
                                                ).then_inc(S["f_hu"], 1)
                                    v.wait_ge(S["f_hu"], NHC * i + c + 1)
                                    _quant_chain(
                                        v, u3_b, s3_s, nb,
                                        hq_bf[i % 2][:, c * QF3:(c + 1) * QF3
                                                     ].rearrange(
                                            "p (nb b) -> p nb b", b=16),
                                        cA3, cB3, usq3, sgn3
                                    ).then_inc(S["f_hq"], 1)


            @blk.scalar
            def _(s):
                s.wait_ge(S["SY"], 16 * NT)

    return nc


def kernel(L0_attn_x, L0_attn_Wq, L0_attn_Wk, L0_attn_Wv, L0_attn_Wo,
           L0_ffn_w_gate, L0_ffn_w_up, L0_ffn_w_down):
    x = np.ascontiguousarray(np.asarray(L0_attn_x, dtype=np.float32))
    B, S_tok, _ = x.shape
    ident = np.eye(128, dtype=ml_dtypes.bfloat16)
    weights = dict(
        wq=np.ascontiguousarray(np.asarray(L0_attn_Wq, dtype=np.float32)),
        wk=np.ascontiguousarray(np.asarray(L0_attn_Wk, dtype=np.float32)),
        wv=np.ascontiguousarray(np.asarray(L0_attn_Wv, dtype=np.float32)),
        wo=np.ascontiguousarray(np.asarray(L0_attn_Wo, dtype=np.float32)),
        ident=ident)
    wg = np.asarray(L0_ffn_w_gate, dtype=np.float32)
    wu_ = np.asarray(L0_ffn_w_up, dtype=np.float32)
    wd = np.asarray(L0_ffn_w_down, dtype=np.float32)
    blocks = []
    for w, nt, nk in ((wg, 22, 8), (wu_, 22, 8), (wd, 8, 22)):
        b4 = w.reshape(nt, 128, nk, 128).transpose(0, 2, 1, 3)
        blocks.append(b4.reshape(nt * nk, 128, 128))
    blocks = np.concatenate(blocks, axis=0)
    nc = build(S_tok)
    in_maps = []
    for c in range(B):
        wsl = np.ascontiguousarray(
            blocks[66 * c:66 * (c + 1)].transpose(1, 0, 2).reshape(128, 66 * 128))
        in_maps.append(dict(x=np.ascontiguousarray(x[c]), wsl=wsl, **weights))
    res = run_bass_kernel_spmd(nc, in_maps, core_ids=list(range(B)))
    return np.stack([np.asarray(r["y"], dtype=np.float32) for r in res.results],
                    axis=0)


# version 15
# speedup vs baseline: 1.3324x; 1.1206x over previous
import contextlib
import numpy as np
import ml_dtypes
import concourse.bass as bass
import concourse.mybir as mybir
from concourse.bass_utils import run_bass_kernel_spmd
from concourse.alu_op_type import AluOpType as A

F32 = mybir.dt.float32
BF16 = mybir.dt.bfloat16
AF = mybir.ActivationFunctionType
AX = mybir.AxisListType.X

D = 1024
FI = 2816
NKC = D // 128
NIC = FI // 128


def _quant_chain(v, u_b, s_small, nb, outcols, cA, cB, ab, sg2):
    F = nb * 16
    v.tensor_tensor(out=ab[:, :F], in0=u_b[:, :F], in1=u_b[:, :F], op=A.mult)
    v.tensor_scalar(out=sg2[:, :F], in0=u_b[:, :F], scalar1=0.0, scalar2=2.0,
                    op0=A.is_gt, op1=A.mult)
    v.tensor_single_scalar(out=cA[:, :F], in_=ab[:, :F], scalar=0.0625,
                           op=A.is_gt)
    for t2 in (0.5625, 1.5625, 3.0625):
        v.scalar_tensor_tensor(out=cA[:, :F], in0=ab[:, :F], scalar=t2,
                               in1=cA[:, :F], op0=A.is_gt, op1=A.add)
    v.tensor_scalar(out=cB[:, :F], in0=ab[:, :F], scalar1=25.0, scalar2=2.0,
                    op0=A.is_gt, op1=A.mult)
    for t2 in (12.25, 6.25):
        v.scalar_tensor_tensor(out=cB[:, :F], in0=ab[:, :F], scalar=t2,
                               in1=cB[:, :F], op0=A.is_gt, op1=A.add)
    v.scalar_tensor_tensor(out=cA[:, :F], in0=cA[:, :F], scalar=0.5,
                           in1=cB[:, :F], op0=A.mult, op1=A.add)
    v.scalar_tensor_tensor(out=cA[:, :F], in0=sg2[:, :F], scalar=-1.0,
                           in1=cA[:, :F], op0=A.add, op1=A.mult)
    sb3 = s_small[:, :nb].unsqueeze(2).broadcast_to([128, nb, 16])
    return v.tensor_tensor(
        out=outcols, in0=cA[:, :F].rearrange("p (nb b) -> p nb b", b=16),
        in1=sb3, op=A.mult)


def _quant_head(v, src3, m_s, s_s, r_s, nb, u_b, vbar):
    vbar(v.tensor_reduce(out=m_s[:, :nb], in_=src3, axis=AX, op=A.max,
                         apply_absolute_value=True))
    vbar(v.tensor_scalar(out=s_s[:, :nb], in0=m_s[:, :nb], scalar1=1.0 / 6.0,
                         scalar2=1e-12, op0=A.mult, op1=A.max))
    vbar(v.reciprocal(out=r_s[:, :nb], in_=s_s[:, :nb]))
    return v.tensor_tensor(
        out=u_b[:, :nb * 16].rearrange("p (nb b) -> p nb b", b=16), in0=src3,
        in1=r_s[:, :nb].unsqueeze(2).broadcast_to([128, nb, 16]), op=A.mult)


def build(S_tok=4096, dbg=False, serial=0):
    NT = S_tok // 128
    nc = bass.Bass("TRN2", target_bir_lowering=False, detect_race_conditions=False)
    KIND = "ExternalOutput" if dbg else "Internal"


    x_d = nc.dram_tensor("x", [S_tok, D], F32, kind="ExternalInput").ap()
    wq_d = nc.dram_tensor("wq", [D, D], F32, kind="ExternalInput").ap()
    wk_d = nc.dram_tensor("wk", [D, D], F32, kind="ExternalInput").ap()
    wv_d = nc.dram_tensor("wv", [D, D], F32, kind="ExternalInput").ap()
    wo_d = nc.dram_tensor("wo", [D, D], F32, kind="ExternalInput").ap()
    wsl_d = nc.dram_tensor("wsl", [128, 66 * 128], F32, kind="ExternalInput").ap()
    id_d = nc.dram_tensor("ident", [128, 128], BF16, kind="ExternalInput").ap()
    y_d = nc.dram_tensor("y", [S_tok, D], F32, kind="ExternalOutput").ap()
    qT_d = nc.dram_tensor("qT_s", [NT, 128, D], BF16, kind=KIND).ap()
    xqT_d = nc.dram_tensor("xqT_s", [NT, 128, D], BF16, kind=KIND).ap()
    v_dd = nc.dram_tensor("v_s", [NT, 128, D], BF16, kind=KIND).ap()
    s_dd = nc.dram_tensor("s_s", [NT, 128, D], BF16, kind=KIND).ap()
    wTl_d = nc.dram_tensor("wTl_s", [128, 66 * 128], BF16, kind="Internal").ap()
    wTg_d = nc.dram_tensor("wTg_s", [8, 128, 66 * 128], BF16, kind=KIND,
                           addr_space="Shared").ap()

    NWU = 6
    WUCOLS = 1408
    wu_of = [[] for _ in range(NT)]
    for u in range(NWU):
        wu_of[min(NT - 1, 2 + u * 5)].append(u)
    NBAT_U = 2
    CBT = NWU * NBAT_U
    CB_tile = []
    acc = 0
    for i in range(NT):
        acc += NBAT_U * len(wu_of[i])
        CB_tile.append(acc)

    st_all = contextlib.ExitStack()
    with st_all:
        E = st_all.enter_context
        SEM = {}
        for name in ("SX", "SWQ", "SWT", "SWW", "SPV", "SPQ", "SPXQ",
                     "c_xread", "c_ssq", "c_var", "c_std", "c_xnbf", "c_usgn",
                     "c_u", "c_xnT", "c_xq", "c_kqv", "c_ktev", "c_qtev",
                     "c_xqtev", "c_xntr", "c_qkv", "c_ktr", "c_qtr", "c_xqtr",
                     "w_u", "w_usgn", "w_q", "w_tr", "w_tev", "CC",
                     "SQR", "SVR", "SWO", "SSW",
                     "p_Sq", "p_Sev", "p_max", "p_bias", "p_exp", "p_pttr",
                     "p_ptev", "p_pv", "p_oev", "p_ottr", "p_otev", "p_smm",
                     "p_sev",
                     "SWR", "SXQ", "SSR", "SX2", "SY",
                     "f_gu", "f_sil", "f_hmul", "f_hu", "f_husgn", "f_hq",
                     "f_httr", "f_htev", "f_down", "f_gsil", "f_comb"):
            SEM[name] = E(nc.semaphore(name))
        S = SEM
        VSY = E(nc.semaphore("VSY"))
        vsy_n = [0]

        def mk_vbar(v):
            def vbar(ins):
                vsy_n[0] += 1
                ins.then_inc(VSY, 1)
                v.wait_ge(VSY, vsy_n[0])
            return vbar

        ident = E(nc.sbuf_tensor("ident_sb", [128, 128], BF16))
        stats = E(nc.sbuf_tensor("stats", [128, 16], F32))
        st_kt = contextlib.ExitStack()
        kT_sb = st_kt.enter_context(
            nc.sbuf_tensor("kT_sb", [128, NKC * S_tok], BF16, side="right"))
        blk = E(nc.Block())

        @blk.sync
        def _(sy):
            sy.dma_start(out=ident[:], in_=id_d).then_inc(S["SX"], 16)

        with contextlib.ExitStack() as st1:
            E1 = st1.enter_context
            wqkv = E1(nc.sbuf_tensor("wqkv", [128, NKC * 3072], BF16))
            xbuf = [E1(nc.sbuf_tensor(f"xbuf{j}", [128, D], F32)) for j in range(2)]
            xc = E1(nc.sbuf_tensor("xc", [128, D], F32))
            xn = E1(nc.sbuf_tensor("xn", [128, D], F32))
            xn_bf = E1(nc.sbuf_tensor("xn_bf", [128, D], BF16))
            xnT = E1(nc.sbuf_tensor("xnT", [128, D], BF16))
            QF = 1408
            u_b = E1(nc.sbuf_tensor("u_b", [128, QF], F32))
            usq = E1(nc.sbuf_tensor("usq", [128, QF], F32))
            sgn = E1(nc.sbuf_tensor("sgn", [128, QF], F32))
            cA = E1(nc.sbuf_tensor("cA", [128, QF], F32))
            cB = E1(nc.sbuf_tensor("cB", [128, QF], F32))
            msr = E1(nc.sbuf_tensor("msr", [128, 3 * 96], F32))
            m_s, s_s, r_s = msr[:, 0:96], msr[:, 96:192], msr[:, 192:288]
            xq_bf = E1(nc.sbuf_tensor("xq_bf", [128, D], BF16))
            q_bf = E1(nc.sbuf_tensor("q_bf", [128, D], BF16))
            k_bf = E1(nc.sbuf_tensor("k_bf", [128, D], BF16))
            v_bf = [E1(nc.sbuf_tensor(f"v_bf{j}", [128, D], BF16)) for j in range(2)]
            qT_bf = [E1(nc.sbuf_tensor(f"qT_bf{j}", [128, D], BF16)) for j in range(2)]
            xqT_bf = [E1(nc.sbuf_tensor(f"xqT_bf{j}", [128, D], BF16)) for j in range(2)]
            wstage = E1(nc.sbuf_tensor("wstage", [128, 1408], F32))
            wq_bf = E1(nc.sbuf_tensor("wq_bf", [128, 1408], BF16))
            wT_st = E1(nc.sbuf_tensor("wT_st", [128, 1408], BF16))
            ps_qkv = [E1(nc.psum_tensor(f"ps_qkv{j}", [128, 512], F32))
                      for j in range(6)]
            ps_t1 = E1(nc.psum_tensor("ps_t1", [128, 1024], BF16))
            ps_t2 = E1(nc.psum_tensor("ps_t2", [128, 1024], BF16))

            @blk.gpsimd
            def _(g):
                for kc in range(NKC):
                    for wi, wsrc in enumerate((wq_d, wk_d, wv_d)):
                        c0 = kc * 3072 + wi * 1024
                        g.dma_start(out=wqkv[:, c0:c0 + 1024],
                                    in_=wsrc[kc * 128:(kc + 1) * 128, :]
                                    ).then_inc(S["SWQ"], 16)

            for i in range(NT):
                wtiles = wu_of[i]

                @blk.sync
                def _(sy, i=i):
                    if i >= 2:
                        sy.wait_ge(S["c_xread"], i - 1)
                    sy.dma_start(out=xbuf[i % 2][:],
                                 in_=x_d[i * 128:(i + 1) * 128, :]
                                 ).then_inc(S["SX"], 16)

                @blk.vector
                def _(v, i=i):
                    if serial & 1 and i > 0:
                        v.wait_ge(S["c_xqtev"], i)
                        if CB_tile[i - 1] > 0:
                            v.wait_ge(S["w_tev"], CB_tile[i - 1])
                    v.wait_ge(S["SX"], 16 * (i + 2))
                    xb = xbuf[i % 2]
                    vbar = mk_vbar(v)
                    vbar(v.tensor_reduce(out=stats[:, 0:1], in_=xb[:], axis=AX,
                                         op=A.add, negate=True))
                    vbar(v.tensor_scalar_mul(out=stats[:, 1:2],
                                             in0=stats[:, 0:1], scalar1=1.0 / D))
                    v.tensor_scalar_add(out=xc[:], in0=xb[:],
                                        scalar1=stats[:, 1:2]
                                        ).then_inc(S["c_xread"], 1)

                @blk.vector
                def _(v, i=i):
                    vbar = mk_vbar(v)
                    vbar(v.scalar_tensor_tensor(out=u_b[:, :D], in0=xc[:],
                                                scalar=1.0, in1=xc[:],
                                                op0=A.mult, op1=A.mult,
                                                accum_out=stats[:, 12:13]))
                    v.tensor_scalar(out=stats[:, 2:3], in0=stats[:, 12:13],
                                    scalar1=1.0 / D, scalar2=1e-5,
                                    op0=A.mult, op1=A.add).then_inc(S["c_var"], 1)

                @blk.scalar
                def _(s, i=i):
                    s.wait_ge(S["c_var"], i + 1)
                    s.activation(out=stats[:, 3:4], in_=stats[:, 2:3],
                                 func=AF.Sqrt).then_inc(S["c_std"], 1)
                    s.wait_ge(S["c_u"], i + 1)
                    s.activation(out=xn_bf[:], in_=xc[:], func=AF.Identity,
                                 scale=stats[:, 4:5]).then_inc(S["c_xnbf"], 1)
                    s.wait_ge(S["c_kqv"], i + 1)
                    s.dma_start(out=v_dd[i], in_=v_bf[i % 2][:]).then_inc(S["SPV"], 16)
                    s.wait_ge(S["c_qtev"], i + 1)
                    s.dma_start(out=qT_d[i], in_=qT_bf[i % 2][:]).then_inc(S["SPQ"], 16)
                    s.wait_ge(S["c_xqtev"], i + 1)
                    s.dma_start(out=xqT_d[i],
                                in_=xqT_bf[i % 2][:]).then_inc(S["SPXQ"], 16)

                @blk.vector
                def _(v, i=i):
                    v.wait_ge(S["c_std"], i + 1)
                    vbar = mk_vbar(v)
                    vbar(v.reciprocal(out=stats[:, 4:5], in_=stats[:, 3:4]))
                    v.tensor_scalar_mul(out=xn[:], in0=xc[:],
                                        scalar1=stats[:, 4:5])
                    x3 = xn[:].rearrange("p (nb b) -> p nb b", b=16)
                    _quant_head(v, x3, m_s, s_s, r_s, 64, u_b,
                                vbar).then_inc(S["c_u"], 1)

                @blk.tensor
                def _(t_, i=i):
                    t_.wait_ge(S["c_xnbf"], i + 1)
                    if i > 0:
                        t_.wait_ge(S["c_qtev"], i)
                        if CB_tile[i - 1] > 0:
                            t_.wait_ge(S["w_tev"], CB_tile[i - 1])
                    for kc in range(NKC):
                        ins = t_.transpose(out=ps_t1[:, kc * 128:(kc + 1) * 128],
                                           in_=xn_bf[:, kc * 128:(kc + 1) * 128],
                                           identity=ident[:])
                    ins.then_inc(S["c_xntr"], 1)

                @blk.vector
                def _(v, i=i):
                    v.wait_ge(S["c_xntr"], i + 1)
                    v.tensor_copy(out=xnT[:], in_=ps_t1[:]).then_inc(S["c_xnT"], 1)
                    v.wait_ge(S["c_u"], i + 1)
                    _quant_chain(v, u_b, s_s, 64,
                                 xq_bf[:].rearrange("p (nb b) -> p nb b", b=16),
                                 cA, cB, usq, sgn).then_inc(S["c_xq"], 1)

                @blk.tensor
                def _(t_, i=i):
                    if i == 0:
                        t_.wait_ge(S["SWQ"], 16 * NKC * 3)
                    t_.wait_ge(S["c_xnT"], i + 1)
                    if i > 0:
                        t_.wait_ge(S["c_kqv"], i)
                    for kc in range(NKC):
                        for ns in range(6):
                            c0 = kc * 3072 + ns * 512
                            ins = t_.matmul(out=ps_qkv[ns][:],
                                            lhsT=xnT[:, kc * 128:(kc + 1) * 128],
                                            rhs=wqkv[:, c0:c0 + 512],
                                            start=(kc == 0), stop=(kc == NKC - 1))
                    ins.then_inc(S["c_qkv"], 1)

                @blk.vector
                def _(v, i=i):
                    v.wait_ge(S["c_qkv"], i + 1)
                    if i >= 2:
                        v.wait_ge(S["SPV"], 16 * (i - 1))
                    v.tensor_copy(out=q_bf[:, 0:512], in_=ps_qkv[0][:])
                    v.tensor_copy(out=q_bf[:, 512:1024], in_=ps_qkv[1][:])
                    v.tensor_copy(out=k_bf[:, 0:512], in_=ps_qkv[2][:])
                    v.tensor_copy(out=k_bf[:, 512:1024], in_=ps_qkv[3][:])
                    v.tensor_copy(out=v_bf[i % 2][:, 0:512], in_=ps_qkv[4][:])
                    v.tensor_copy(out=v_bf[i % 2][:, 512:1024],
                                  in_=ps_qkv[5][:]).then_inc(S["c_kqv"], 1)

                @blk.tensor
                def _(t_, i=i):
                    t_.wait_ge(S["c_kqv"], i + 1)
                    if i > 0:
                        t_.wait_ge(S["c_xqtev"], i)
                    for kc in range(NKC):
                        ins = t_.transpose(out=ps_t2[:, kc * 128:(kc + 1) * 128],
                                           in_=k_bf[:, kc * 128:(kc + 1) * 128],
                                           identity=ident[:])
                    ins.then_inc(S["c_ktr"], 1)

                @blk.vector
                def _(v, i=i):
                    v.wait_ge(S["c_ktr"], i + 1)
                    kdst = kT_sb[:].rearrange("p (dc s) -> p dc s", s=S_tok)[
                        :, :, i * 128:(i + 1) * 128]
                    v.tensor_copy(out=kdst, in_=ps_t2[:].rearrange(
                        "p (dc j) -> p dc j", j=128)).then_inc(S["c_ktev"], 1)

                @blk.tensor
                def _(t_, i=i):
                    for kc in range(NKC):
                        ins = t_.transpose(out=ps_t1[:, kc * 128:(kc + 1) * 128],
                                           in_=q_bf[:, kc * 128:(kc + 1) * 128],
                                           identity=ident[:])
                    ins.then_inc(S["c_qtr"], 1)

                @blk.vector
                def _(v, i=i):
                    v.wait_ge(S["c_qtr"], i + 1)
                    if i >= 2:
                        v.wait_ge(S["SPQ"], 16 * (i - 1))
                    v.tensor_copy(out=qT_bf[i % 2][:],
                                  in_=ps_t1[:]).then_inc(S["c_qtev"], 1)

                @blk.tensor
                def _(t_, i=i):
                    t_.wait_ge(S["c_ktev"], i + 1)
                    t_.wait_ge(S["c_xq"], i + 1)
                    for kc in range(NKC):
                        ins = t_.transpose(out=ps_t2[:, kc * 128:(kc + 1) * 128],
                                           in_=xq_bf[:, kc * 128:(kc + 1) * 128],
                                           identity=ident[:])
                    ins.then_inc(S["c_xqtr"], 1)

                @blk.vector
                def _(v, i=i):
                    v.wait_ge(S["c_xqtr"], i + 1)
                    if i >= 2:
                        v.wait_ge(S["SPXQ"], 16 * (i - 1))
                    v.tensor_copy(out=xqT_bf[i % 2][:],
                                  in_=ps_t2[:]).then_inc(S["c_xqtev"], 1)

                for u in wtiles:
                    @blk.gpsimd
                    def _(g, u=u):
                        if u >= 1:
                            g.wait_ge(S["w_u"], u)
                        g.dma_start(out=wstage[:, :WUCOLS],
                                    in_=wsl_d[:, u * WUCOLS:(u + 1) * WUCOLS]
                                    ).then_inc(S["SWT"], 16)

                    @blk.vector
                    def _(v, u=u):
                        v.wait_ge(S["SWT"], 16 * (u + 1))
                        if u >= 1:
                            v.wait_ge(S["w_tr"], NBAT_U * u)
                        w3 = wstage[:, :WUCOLS].rearrange(
                            "p (nb b) -> p nb b", b=16)
                        _quant_head(v, w3, m_s, s_s, r_s, WUCOLS // 16, u_b,
                                    mk_vbar(v)).then_inc(S["w_u"], 1)
                        _quant_chain(
                            v, u_b, s_s, WUCOLS // 16,
                            wq_bf[:, :WUCOLS].rearrange("p (nb b) -> p nb b", b=16),
                            cA, cB, usq, sgn).then_inc(S["w_q"], 1)

                    for b in range(NBAT_U):
                        lo, hi = b * 8, min(11, b * 8 + 8)

                        @blk.tensor
                        def _(t_, u=u, b=b, lo=lo, hi=hi, i=i):
                            t_.wait_ge(S["w_q"], u + 1)
                            if b == 0 and u == wtiles[0]:
                                t_.wait_ge(S["c_qtev"], i + 1)
                            if NBAT_U * u + b > 0:
                                t_.wait_ge(S["w_tev"], NBAT_U * u + b)
                            for kk in range(lo, hi):
                                ins = t_.transpose(
                                    out=ps_t1[:, (kk - lo) * 128:(kk - lo + 1) * 128],
                                    in_=wq_bf[:, kk * 128:(kk + 1) * 128],
                                    identity=ident[:])
                            ins.then_inc(S["w_tr"], 1)

                        @blk.vector
                        def _(v, u=u, b=b, lo=lo, hi=hi):
                            v.wait_ge(S["w_tr"], NBAT_U * u + b + 1)
                            if u >= 1 and b == 0:
                                v.wait_ge(S["SWW"], 16 * u)
                            v.tensor_copy(
                                out=wT_st[:, lo * 128:hi * 128],
                                in_=ps_t1[:, 0:(hi - lo) * 128]
                            ).then_inc(S["w_tev"], 1)

                    @blk.gpsimd
                    def _(g, u=u):
                        g.wait_ge(S["w_tev"], NBAT_U * (u + 1))
                        g.dma_start(out=wTl_d[:, u * WUCOLS:(u + 1) * WUCOLS],
                                    in_=wT_st[:, :WUCOLS]).then_inc(S["SWW"], 16)

            @blk.gpsimd
            def _(g):
                g.wait_ge(S["SWW"], 16 * NWU)
                g.collective_compute(
                    "AllGather", A.bypass,
                    replica_groups=[list(range(8))],
                    ins=[wTl_d], outs=[wTg_d.rearrange("r p c -> (r p) c")],
                ).then_inc(S["CC"], 1)

        with contextlib.ExitStack() as st2:
            E2 = st2.enter_context
            v_sb = E2(nc.sbuf_tensor("v_sb", [128, S_tok * NKC], BF16))
            wo_sb = E2(nc.sbuf_tensor("wo_sb", [128, NKC * 1024], BF16))
            S_row = E2(nc.sbuf_tensor("S_row", [128, S_tok], F32))
            P_bf = [E2(nc.sbuf_tensor(f"P_bf{j}", [128, S_tok], BF16))
                    for j in range(2)]
            PT_sb = E2(nc.sbuf_tensor("PT_sb", [128, S_tok], BF16))
            qT_t = [E2(nc.sbuf_tensor(f"qT_t{j}", [128, D], BF16)) for j in range(2)]
            o_bf = E2(nc.sbuf_tensor("o_bf", [128, D], BF16))
            oT_sb = E2(nc.sbuf_tensor("oT_sb", [128, D], BF16))
            s_bf = [E2(nc.sbuf_tensor(f"s_bf{j}", [128, D], BF16)) for j in range(2)]
            ps_s = [E2(nc.psum_tensor(f"ps_s{j}", [128, 1024], F32)) for j in range(2)]
            ps_pt = E2(nc.psum_tensor("ps_pt", [128, 1024], BF16))
            ps_o = E2(nc.psum_tensor("ps_o", [128, 1024], F32))
            ps_ot = E2(nc.psum_tensor("ps_ot", [128, 1024], BF16))
            NQ = S_tok // 1024
            NJC = S_tok // 128
            NPB = NJC // 8


            @blk.gpsimd
            def _(g):
                g.wait_ge(S["c_xqtev"], NT)
                if CBT:
                    g.wait_ge(S["w_tev"], CBT)
                    g.wait_ge(S["w_tr"], CBT)
                g.wait_ge(S["c_xqtr"], NT)
                g.wait_ge(S["SPV"], 16 * NT)
                g.wait_ge(S["SPQ"], 16 * NT)
                g.wait_ge(S["SPXQ"], 16 * NT)
                for kc in range(NKC):
                    g.dma_start(out=wo_sb[:, kc * 1024:(kc + 1) * 1024],
                                in_=wo_d[kc * 128:(kc + 1) * 128, :]
                                ).then_inc(S["SWO"], 16)

            @blk.sync
            def _(sy):
                sy.wait_ge(S["SPV"], 16 * NT)
                sy.wait_ge(S["SPQ"], 16 * NT)
                sy.wait_ge(S["SPXQ"], 16 * NT)
                sy.wait_ge(S["c_xqtev"], NT)
                if CBT:
                    sy.wait_ge(S["w_tev"], CBT)
                    sy.wait_ge(S["w_tr"], CBT)
                sy.wait_ge(S["c_xqtr"], NT)
                for j in range(NT):
                    sy.dma_start(
                        out=v_sb[:].rearrange("p (j c) -> p j c", c=1024)[:, j, :],
                        in_=v_dd[j]).then_inc(S["SVR"], 16)

            for i in range(NT + 1):
                if i < NT:
                    @blk.sync
                    def _(sy, i=i):
                        sy.wait_ge(S["SPQ"], 16 * (i + 1))
                        if i >= 2:
                            sy.wait_ge(S["p_smm"], i - 1)
                        sy.dma_start(out=qT_t[i % 2][:],
                                     in_=qT_d[i]).then_inc(S["SQR"], 16)

                    for qq in range(NQ):
                        @blk.tensor
                        def _(t_, i=i, qq=qq):
                            if serial & 2 and qq == 0 and i > 0:
                                t_.wait_ge(S["p_sev"], i)
                            if i == 0 and qq == 0:
                                t_.wait_ge(S["SVR"], 16 * NT)
                                t_.wait_ge(S["c_ktev"], NT)
                                t_.wait_ge(S["c_xqtev"], NT)
                                if CBT:
                                    t_.wait_ge(S["w_tev"], CBT)
                            if qq == 0:
                                t_.wait_ge(S["SQR"], 16 * (i + 1))
                            prev = NQ * i + qq - (2 if NQ >= 2 else 1) + 1
                            if prev > 0:
                                t_.wait_ge(S["p_Sev"], prev)
                                t_.wait_ge(S["p_max"], prev)
                            for dc in range(NKC):
                                for sl in range(2):
                                    col = dc * S_tok + qq * 1024 + sl * 512
                                    ins = t_.matmul(
                                        out=ps_s[qq % 2][:, sl * 512:(sl + 1) * 512],
                                        lhsT=qT_t[i % 2][:, dc * 128:(dc + 1) * 128],
                                        rhs=kT_sb[:, col:col + 512],
                                        start=(dc == 0), stop=(dc == NKC - 1))
                            ins.then_inc(S["p_Sq"], 1)

                        @blk.scalar
                        def _(s, i=i, qq=qq):
                            s.wait_ge(S["p_Sq"], NQ * i + qq + 1)
                            s.activation(out=S_row[:, qq * 1024:(qq + 1) * 1024],
                                         in_=ps_s[qq % 2][:],
                                         func=AF.Identity).then_inc(S["p_Sev"], 1)

                        @blk.vector
                        def _(v, i=i, qq=qq):
                            v.wait_ge(S["p_Sq"], NQ * i + qq + 1)
                            v.tensor_reduce(out=stats[:, 7 + qq:8 + qq],
                                            in_=ps_s[qq % 2][:], axis=AX, op=A.max,
                                            negate=True).then_inc(S["p_max"], 1)

                    @blk.vector
                    def _(v, i=i):
                        v.wait_ge(S["p_max"], NQ * (i + 1))
                        vbar = mk_vbar(v)
                        if NQ >= 2:
                            vbar(v.tensor_tensor(out=stats[:, 7:8],
                                                 in0=stats[:, 7:8],
                                                 in1=stats[:, 8:9], op=A.min))
                        if NQ >= 4:
                            vbar(v.tensor_tensor(out=stats[:, 9:10],
                                                 in0=stats[:, 9:10],
                                                 in1=stats[:, 10:11], op=A.min))
                            vbar(v.tensor_tensor(out=stats[:, 7:8],
                                                 in0=stats[:, 7:8],
                                                 in1=stats[:, 9:10], op=A.min))
                        bcol = 11 + 2 * (i % 2)
                        v.tensor_scalar_mul(out=stats[:, bcol:bcol + 1],
                                            in0=stats[:, 7:8],
                                            scalar1=8.0).then_inc(S["p_bias"], 1)

                    @blk.scalar
                    def _(s, i=i):
                        s.wait_ge(S["p_bias"], i + 1)
                        if i >= 2:
                            s.wait_ge(S["p_pttr"], NPB * (i - 1))
                        bcol = 11 + 2 * (i % 2)
                        rcol = 5 + 7 * (i % 2)
                        s.activation(out=P_bf[i % 2][:], in_=S_row[:], func=AF.Exp,
                                     bias=stats[:, bcol:bcol + 1], scale=8.0,
                                     accum_out=stats[:, rcol:rcol + 1]
                                     ).then_inc(S["p_exp"], 1)

                if i >= 1:
                    j = i - 1
                    for b in range(NPB):
                        @blk.tensor
                        def _(t_, j=j, b=b):
                            if b == 0:
                                t_.wait_ge(S["p_exp"], j + 1)
                            prev = NPB * j + b - 1
                            if prev > 0:
                                t_.wait_ge(S["p_ptev"], prev)
                            if b == 1 and j > 0:
                                t_.wait_ge(S["p_otev"], j)
                            pbank = ps_pt if b % 2 == 0 else ps_ot
                            for k in range(8):
                                jc = b * 8 + k
                                ins = t_.transpose(
                                    out=pbank[:, k * 128:(k + 1) * 128],
                                    in_=P_bf[j % 2][:, jc * 128:(jc + 1) * 128],
                                    identity=ident[:])
                            ins.then_inc(S["p_pttr"], 1)

                        @blk.vector
                        def _(v, j=j, b=b):
                            v.wait_ge(S["p_pttr"], NPB * j + b + 1)
                            pbank = ps_pt if b % 2 == 0 else ps_ot
                            v.tensor_copy(out=PT_sb[:, b * 1024:(b + 1) * 1024],
                                          in_=pbank[:]).then_inc(S["p_ptev"], 1)

                    @blk.tensor
                    def _(t_, j=j):
                        if j > 0:
                            t_.wait_ge(S["p_sev"], j)
                        for b in range(NPB):
                            t_.wait_ge(S["p_ptev"], NPB * j + b + 1)
                            for k in range(8):
                                jc = b * 8 + k
                                for dh in range(2):
                                    col = jc * 1024 + dh * 512
                                    ins = t_.matmul(
                                        out=ps_o[:, dh * 512:(dh + 1) * 512],
                                        lhsT=PT_sb[:, jc * 128:(jc + 1) * 128],
                                        rhs=v_sb[:, col:col + 512],
                                        start=(jc == 0), stop=(jc == NJC - 1))
                        ins.then_inc(S["p_pv"], 1)

                    @blk.vector
                    def _(v, j=j):
                        v.wait_ge(S["p_exp"], j + 1)
                        vbar = mk_vbar(v)
                        rcol = 5 + 7 * (j % 2)
                        vbar(v.reciprocal(out=stats[:, 6:7],
                                          in_=stats[:, rcol:rcol + 1]))
                        v.wait_ge(S["p_pv"], j + 1)
                        v.tensor_scalar_mul(
                            out=o_bf[:], in0=ps_o[:],
                            scalar1=stats[:, 6:7]).then_inc(S["p_oev"], 1)

                    @blk.tensor
                    def _(t_, j=j):
                        t_.wait_ge(S["p_oev"], j + 1)
                        t_.wait_ge(S["p_ptev"], NPB * (j + 1))
                        if j > 0:
                            t_.wait_ge(S["p_otev"], j)
                        for kc in range(NKC):
                            ins = t_.transpose(out=ps_ot[:, kc * 128:(kc + 1) * 128],
                                               in_=o_bf[:, kc * 128:(kc + 1) * 128],
                                               identity=ident[:])
                        ins.then_inc(S["p_ottr"], 1)

                    @blk.vector
                    def _(v, j=j):
                        v.wait_ge(S["p_ottr"], j + 1)
                        v.tensor_copy(out=oT_sb[:],
                                      in_=ps_ot[:]).then_inc(S["p_otev"], 1)

                    @blk.tensor
                    def _(t_, j=j):
                        if j == 0:
                            t_.wait_ge(S["SWO"], 16 * NKC)
                        t_.wait_ge(S["p_otev"], j + 1)
                        for dc in range(NKC):
                            for sl in range(2):
                                col = dc * 1024 + sl * 512
                                ins = t_.matmul(
                                    out=ps_o[:, sl * 512:(sl + 1) * 512],
                                    lhsT=oT_sb[:, dc * 128:(dc + 1) * 128],
                                    rhs=wo_sb[:, col:col + 512],
                                    start=(dc == 0), stop=(dc == NKC - 1))
                        ins.then_inc(S["p_smm"], 1)

                    @blk.vector
                    def _(v, j=j):
                        v.wait_ge(S["p_smm"], j + 1)
                        if j >= 2:
                            v.wait_ge(S["SSW"], 16 * (j - 1))
                        v.tensor_copy(out=s_bf[j % 2][:],
                                      in_=ps_o[:]).then_inc(S["p_sev"], 1)

                    @blk.scalar
                    def _(s, j=j):
                        s.wait_ge(S["p_sev"], j + 1)
                        s.dma_start(out=s_dd[j],
                                    in_=s_bf[j % 2][:]).then_inc(S["SSW"], 16)

        st_kt.close()

        with contextlib.ExitStack() as st3:
            E3 = st3.enter_context
            wT_all = E3(nc.sbuf_tensor("wT_all", [128, 528 * 128], BF16))
            xqT_t = [E3(nc.sbuf_tensor(f"xqT_t{j}", [128, D], BF16)) for j in range(2)]
            s_t = [E3(nc.sbuf_tensor(f"s_t{j}", [128, D], BF16)) for j in range(2)]
            x2 = [E3(nc.sbuf_tensor(f"x2_{j}", [128, D], F32)) for j in range(2)]
            h_f = E3(nc.sbuf_tensor("h_f", [128, FI], F32))
            hq_bf = [E3(nc.sbuf_tensor(f"hq_bf{j}", [128, FI], BF16))
                     for j in range(2)]
            hqT_sb = E3(nc.sbuf_tensor("hqT_sb", [128, FI], BF16))
            hg_t = E3(nc.sbuf_tensor("hg_t", [128, 512], F32))
            g_f = E3(nc.sbuf_tensor("g_f", [128, D], F32))
            out_sb = [E3(nc.sbuf_tensor(f"out_sb{j}", [128, D], F32))
                      for j in range(2)]
            QF3 = 704
            NHC = FI // QF3
            u3_b = E3(nc.sbuf_tensor("u3_b", [128, QF3], F32))
            usq3 = E3(nc.sbuf_tensor("usq3", [128, QF3], F32))
            sgn3 = E3(nc.sbuf_tensor("sgn3", [128, QF3], F32))
            cA3 = E3(nc.sbuf_tensor("cA3", [128, QF3], F32))
            cB3 = E3(nc.sbuf_tensor("cB3", [128, QF3], F32))
            msr3 = E3(nc.sbuf_tensor("msr3", [128, 3 * 48], F32))
            m3_s, s3_s, r3_s = msr3[:, 0:48], msr3[:, 48:96], msr3[:, 96:144]
            ps_g = [E3(nc.psum_tensor(f"ps_g{j}", [128, 512], F32)) for j in range(4)]
            ps_ht = E3(nc.psum_tensor("ps_ht", [128, 1024], BF16))
            ps_d = E3(nc.psum_tensor("ps_d", [128, 1024], F32))

            nwin = []
            off = 0
            while off < FI:
                w = min(512, FI - off)
                nwin.append((off, w))
                off += w
            NCW = len(nwin)
            NB3 = (NIC + 7) // 8
            wgT3 = wT_all[:, 0:176 * 128].rearrange(
                "p (nt kc j) -> p nt kc j", kc=NKC, j=128)
            wuT3 = wT_all[:, 176 * 128:352 * 128].rearrange(
                "p (nt kc j) -> p nt kc j", kc=NKC, j=128)
            wdT3 = wT_all[:, 352 * 128:528 * 128].rearrange(
                "p (dt ic j) -> p dt ic j", ic=NIC, j=128)
            hq_need = [min(NHC, -(-((b * 8 + 8) * 128) // QF3)) for b in range(NB3)]

            @blk.gpsimd
            def _(g):
                g.wait_ge(S["CC"], 1)
                g.wait_ge(S["p_smm"], NT)
                g.wait_ge(S["p_sev"], NT)
                g.wait_ge(S["SSW"], 16 * NT)
                for r in range(8):
                    g.dma_start(out=wT_all[:, r * 66 * 128:(r + 1) * 66 * 128],
                                in_=wTg_d[r]).then_inc(S["SWR"], 16)

            for i in range(NT + 1):
                if i >= 1:
                    j = i - 1
                    for b in range(NB3):
                        lo, hi = b * 8, min(NIC, b * 8 + 8)

                        @blk.tensor
                        def _(t_, j=j, b=b, lo=lo, hi=hi):
                            t_.wait_ge(S["f_hq"], NHC * j + hq_need[b])
                            prev = NB3 * j + b
                            if prev > 0:
                                t_.wait_ge(S["f_htev"], prev)
                            for kk in range(lo, hi):
                                ins = t_.transpose(
                                    out=ps_ht[:, (kk - lo) * 128:(kk - lo + 1) * 128],
                                    in_=hq_bf[j % 2][:, kk * 128:(kk + 1) * 128],
                                    identity=ident[:])
                            ins.then_inc(S["f_httr"], 1)

                        @blk.vector
                        def _(v, j=j, b=b, lo=lo, hi=hi):
                            v.wait_ge(S["f_httr"], NB3 * j + b + 1)
                            v.tensor_copy(out=hqT_sb[:, lo * 128:hi * 128],
                                          in_=ps_ht[:, 0:(hi - lo) * 128]
                                          ).then_inc(S["f_htev"], 1)

                if i < NT:
                    @blk.sync
                    def _(sy, i=i):
                        if i == 0:
                            sy.wait_ge(S["p_smm"], NT)
                            sy.wait_ge(S["p_sev"], NT)
                        sy.wait_ge(S["SPXQ"], 16 * (i + 1))
                        if i >= 2:
                            sy.wait_ge(S["f_gu"], NCW * (i - 1))
                        sy.dma_start(out=xqT_t[i % 2][:],
                                     in_=xqT_d[i]).then_inc(S["SXQ"], 16)
                        sy.wait_ge(S["SSW"], 16 * (i + 1))
                        if i >= 2:
                            sy.wait_ge(S["f_comb"], i - 1)
                        sy.dma_start(out=s_t[i % 2][:],
                                     in_=s_dd[i]).then_inc(S["SSR"], 16)
                        sy.dma_start(out=x2[i % 2][:],
                                     in_=x_d[i * 128:(i + 1) * 128, :]
                                     ).then_inc(S["SX2"], 16)

                    for ci, (noff, nw) in enumerate(nwin[:3]):
                        pair = ci % 2

                        @blk.tensor
                        def _(t_, i=i, ci=ci, noff=noff, nw=nw, pair=pair):
                            if serial & 4 and ci == 0 and i > 0:
                                t_.wait_ge(S["f_comb"], i)
                            if i == 0 and ci == 0:
                                t_.wait_ge(S["SWR"], 16 * 8)
                                t_.wait_ge(S["SSW"], 16 * NT)
                                t_.wait_ge(S["p_sev"], NT)
                            if ci == 0:
                                t_.wait_ge(S["SXQ"], 16 * (i + 1))
                            prev = NCW * i + ci - 1
                            if prev > 0:
                                t_.wait_ge(S["f_hmul"], prev)
                                t_.wait_ge(S["f_sil"], prev)
                            nt0, ntn = noff // 128, nw // 128
                            for kc in range(NKC):
                                t_.matmul(out=ps_g[2 * pair][:, :nw],
                                          lhsT=xqT_t[i % 2][:, kc * 128:(kc + 1) * 128],
                                          rhs=wgT3[:, nt0:nt0 + ntn, kc, :],
                                          start=(kc == 0), stop=(kc == NKC - 1))
                                ins = t_.matmul(
                                    out=ps_g[2 * pair + 1][:, :nw],
                                    lhsT=xqT_t[i % 2][:, kc * 128:(kc + 1) * 128],
                                    rhs=wuT3[:, nt0:nt0 + ntn, kc, :],
                                    start=(kc == 0), stop=(kc == NKC - 1))
                            ins.then_inc(S["f_gu"], 1)

                        @blk.scalar
                        def _(s, i=i, ci=ci, nw=nw, pair=pair):
                            s.wait_ge(S["f_gu"], NCW * i + ci + 1)
                            prev = NCW * i + ci
                            if prev > 0:
                                s.wait_ge(S["f_hmul"], prev)
                            s.activation(out=hg_t[:, :nw], in_=ps_g[2 * pair][:, :nw],
                                         func=AF.Silu).then_inc(S["f_sil"], 1)

                        @blk.vector
                        def _(v, i=i, ci=ci, noff=noff, nw=nw, pair=pair):
                            v.wait_ge(S["f_sil"], NCW * i + ci + 1)
                            v.tensor_tensor(out=h_f[:, noff:noff + nw],
                                            in0=hg_t[:, :nw],
                                            in1=ps_g[2 * pair + 1][:, :nw],
                                            op=A.mult).then_inc(S["f_hmul"], 1)

                        for c in range(NHC):
                            if -(-((c + 1) * QF3) // 512) == ci + 1:
                                @blk.vector
                                def _(v, i=i, c=c, ci=ci):
                                    if c == 0 and i >= 2:
                                        v.wait_ge(S["f_httr"], NB3 * (i - 1))
                                    v.wait_ge(S["f_hmul"], NCW * i + ci + 1)
                                    nb = QF3 // 16
                                    h3 = h_f[:, c * QF3:(c + 1) * QF3].rearrange(
                                        "p (nb b) -> p nb b", b=16)
                                    _quant_head(v, h3, m3_s, s3_s, r3_s, nb,
                                                u3_b, mk_vbar(v)
                                                ).then_inc(S["f_hu"], 1)
                                    v.wait_ge(S["f_hu"], NHC * i + c + 1)
                                    _quant_chain(
                                        v, u3_b, s3_s, nb,
                                        hq_bf[i % 2][:, c * QF3:(c + 1) * QF3
                                                     ].rearrange(
                                            "p (nb b) -> p nb b", b=16),
                                        cA3, cB3, usq3, sgn3
                                    ).then_inc(S["f_hq"], 1)


                if i >= 1:
                    j = i - 1
                    @blk.tensor
                    def _(t_, j=j):
                        t_.wait_ge(S["f_htev"], NB3 * (j + 1))
                        if j > 0:
                            t_.wait_ge(S["f_gsil"], j)
                        for ic in range(NIC):
                            for sl in range(2):
                                ins = t_.matmul(
                                    out=ps_d[:, sl * 512:(sl + 1) * 512],
                                    lhsT=hqT_sb[:, ic * 128:(ic + 1) * 128],
                                    rhs=wdT3[:, sl * 4:sl * 4 + 4, ic, :],
                                    start=(ic == 0), stop=(ic == NIC - 1))
                        ins.then_inc(S["f_down"], 1)

                    @blk.scalar
                    def _(s, j=j):
                        s.wait_ge(S["f_down"], j + 1)
                        if j > 0:
                            s.wait_ge(S["f_comb"], j)
                        s.activation(out=g_f[:], in_=ps_d[:],
                                     func=AF.Silu).then_inc(S["f_gsil"], 1)

                if i >= 1:
                    j = i - 1
                if i < NT:
                    for ci, (noff, nw) in list(enumerate(nwin))[3:]:
                        pair = ci % 2

                        @blk.tensor
                        def _(t_, i=i, ci=ci, noff=noff, nw=nw, pair=pair):
                            if serial & 4 and ci == 0 and i > 0:
                                t_.wait_ge(S["f_comb"], i)
                            if i == 0 and ci == 0:
                                t_.wait_ge(S["SWR"], 16 * 8)
                                t_.wait_ge(S["SSW"], 16 * NT)
                                t_.wait_ge(S["p_sev"], NT)
                            if ci == 0:
                                t_.wait_ge(S["SXQ"], 16 * (i + 1))
                            prev = NCW * i + ci - 1
                            if prev > 0:
                                t_.wait_ge(S["f_hmul"], prev)
                                t_.wait_ge(S["f_sil"], prev)
                            nt0, ntn = noff // 128, nw // 128
                            for kc in range(NKC):
                                t_.matmul(out=ps_g[2 * pair][:, :nw],
                                          lhsT=xqT_t[i % 2][:, kc * 128:(kc + 1) * 128],
                                          rhs=wgT3[:, nt0:nt0 + ntn, kc, :],
                                          start=(kc == 0), stop=(kc == NKC - 1))
                                ins = t_.matmul(
                                    out=ps_g[2 * pair + 1][:, :nw],
                                    lhsT=xqT_t[i % 2][:, kc * 128:(kc + 1) * 128],
                                    rhs=wuT3[:, nt0:nt0 + ntn, kc, :],
                                    start=(kc == 0), stop=(kc == NKC - 1))
                            ins.then_inc(S["f_gu"], 1)

                        @blk.scalar
                        def _(s, i=i, ci=ci, nw=nw, pair=pair):
                            s.wait_ge(S["f_gu"], NCW * i + ci + 1)
                            prev = NCW * i + ci
                            if prev > 0:
                                s.wait_ge(S["f_hmul"], prev)
                            s.activation(out=hg_t[:, :nw], in_=ps_g[2 * pair][:, :nw],
                                         func=AF.Silu).then_inc(S["f_sil"], 1)

                        @blk.vector
                        def _(v, i=i, ci=ci, noff=noff, nw=nw, pair=pair):
                            v.wait_ge(S["f_sil"], NCW * i + ci + 1)
                            v.tensor_tensor(out=h_f[:, noff:noff + nw],
                                            in0=hg_t[:, :nw],
                                            in1=ps_g[2 * pair + 1][:, :nw],
                                            op=A.mult).then_inc(S["f_hmul"], 1)

                        for c in range(NHC):
                            if -(-((c + 1) * QF3) // 512) == ci + 1:
                                @blk.vector
                                def _(v, i=i, c=c, ci=ci):
                                    if c == 0 and i >= 2:
                                        v.wait_ge(S["f_httr"], NB3 * (i - 1))
                                    v.wait_ge(S["f_hmul"], NCW * i + ci + 1)
                                    nb = QF3 // 16
                                    h3 = h_f[:, c * QF3:(c + 1) * QF3].rearrange(
                                        "p (nb b) -> p nb b", b=16)
                                    _quant_head(v, h3, m3_s, s3_s, r3_s, nb,
                                                u3_b, mk_vbar(v)
                                                ).then_inc(S["f_hu"], 1)
                                    v.wait_ge(S["f_hu"], NHC * i + c + 1)
                                    _quant_chain(
                                        v, u3_b, s3_s, nb,
                                        hq_bf[i % 2][:, c * QF3:(c + 1) * QF3
                                                     ].rearrange(
                                            "p (nb b) -> p nb b", b=16),
                                        cA3, cB3, usq3, sgn3
                                    ).then_inc(S["f_hq"], 1)


                if i >= 1:
                    j = i - 1
                    @blk.vector
                    def _(v, j=j):
                        v.wait_ge(S["f_gsil"], j + 1)
                        v.wait_ge(S["SSR"], 16 * (j + 1))
                        v.wait_ge(S["SX2"], 16 * (j + 1))
                        if j >= 2:
                            v.wait_ge(S["SY"], 16 * (j - 1))
                        v.tensor_tensor(out=g_f[:], in0=g_f[:], in1=s_t[j % 2][:],
                                        op=A.mult)
                        v.tensor_tensor(out=out_sb[j % 2][:], in0=g_f[:],
                                        in1=x2[j % 2][:], op=A.add
                                        ).then_inc(S["f_comb"], 1)

                    @blk.scalar
                    def _(s, j=j):
                        s.wait_ge(S["f_comb"], j + 1)
                        s.dma_start(out=y_d[j * 128:(j + 1) * 128, :],
                                    in_=out_sb[j % 2][:]).then_inc(S["SY"], 16)

            @blk.scalar
            def _(s):
                s.wait_ge(S["SY"], 16 * NT)

    return nc


def kernel(L0_attn_x, L0_attn_Wq, L0_attn_Wk, L0_attn_Wv, L0_attn_Wo,
           L0_ffn_w_gate, L0_ffn_w_up, L0_ffn_w_down):
    x = np.ascontiguousarray(np.asarray(L0_attn_x, dtype=np.float32))
    B, S_tok, _ = x.shape
    ident = np.eye(128, dtype=ml_dtypes.bfloat16)
    weights = dict(
        wq=np.ascontiguousarray(np.asarray(L0_attn_Wq, dtype=np.float32)),
        wk=np.ascontiguousarray(np.asarray(L0_attn_Wk, dtype=np.float32)),
        wv=np.ascontiguousarray(np.asarray(L0_attn_Wv, dtype=np.float32)),
        wo=np.ascontiguousarray(np.asarray(L0_attn_Wo, dtype=np.float32)),
        ident=ident)
    wg = np.asarray(L0_ffn_w_gate, dtype=np.float32)
    wu_ = np.asarray(L0_ffn_w_up, dtype=np.float32)
    wd = np.asarray(L0_ffn_w_down, dtype=np.float32)
    blocks = []
    for w, nt, nk in ((wg, 22, 8), (wu_, 22, 8), (wd, 8, 22)):
        b4 = w.reshape(nt, 128, nk, 128).transpose(0, 2, 1, 3)
        blocks.append(b4.reshape(nt * nk, 128, 128))
    blocks = np.concatenate(blocks, axis=0)
    nc = build(S_tok)
    in_maps = []
    for c in range(B):
        wsl = np.ascontiguousarray(
            blocks[66 * c:66 * (c + 1)].transpose(1, 0, 2).reshape(128, 66 * 128))
        in_maps.append(dict(x=np.ascontiguousarray(x[c]), wsl=wsl, **weights))
    res = run_bass_kernel_spmd(nc, in_maps, core_ids=list(range(B)))
    return np.stack([np.asarray(r["y"], dtype=np.float32) for r in res.results],
                    axis=0)
